# version 02
# baseline (speedup 1.0000x reference)
import numpy as np
import ml_dtypes
import concourse.bass as bass
import concourse.mybir as mybir
from concourse.bass_utils import run_bass_kernel_spmd
from contextlib import ExitStack

F32 = mybir.dt.float32
BF16 = mybir.dt.bfloat16
U32 = mybir.dt.uint32
AF = mybir.ActivationFunctionType
ALU = mybir.AluOpType
AX = mybir.AxisListType

NCORES = 8
H = 2048
HKT = 16
NE = 32
DFF = 768
NH = 16
HD = 128
B = 2
S = 8192
TPC = 2048
THALF = 1024
NTT = 4
CEXP = 20.0
EPS = 1e-5
SCALE = 1.0 / np.sqrt(128.0)

ENGINES = ("sync", "gpsimd", "tensor", "scalar", "vector")


class Sched:

    def __init__(self, sems):
        self.ops = {e: [] for e in ENGINES}
        self.cnt = {name: 0 for name in sems}
        self.sems = sems

    def op(self, eng, fn):
        self.ops[eng].append(fn)

    def inc(self, name, by):
        self.cnt[name] += by
        return self.cnt[name]

    def dma(self, eng, sem_name, out_ap, in_ap, **kw):
        semh = self.sems[sem_name]
        tgt = self.inc(sem_name, 16)
        self.op(eng, lambda e, o=out_ap, i=in_ap, kw=kw: e.dma_start(out=o, in_=i, **kw).then_inc(semh, 16))
        return tgt

    def wait(self, eng, sem_name, val):
        if val is None or val <= 0:
            return
        semh = self.sems[sem_name]
        self.op(eng, lambda e, v=val: e.wait_ge(semh, v))

    def compute(self, eng, sem_name, fn):
        semh = self.sems[sem_name]
        tgt = self.inc(sem_name, 1)
        self.op(eng, lambda e, f=fn: f(e).then_inc(semh, 1))
        return tgt

    def run(self, eng, fn):
        self.op(eng, fn)

    def barrier(self):
        semh = self.sems["bar"]
        tgt = self.inc("bar", len(ENGINES))
        for eng in ENGINES:
            self.op(eng, lambda e: e.nop().then_inc(semh, 1))
            self.op(eng, lambda e, v=tgt: e.wait_ge(semh, v))
        return tgt


def build_program(stop_after=None, dev_dump=()):
    nc = bass.Bass("TRN2", target_bir_lowering=False)

    x0 = nc.dram_tensor("x0", [H, TPC], F32, kind="ExternalInput")
    gu_t = nc.dram_tensor("gu_t", [NE, 12, 128, HKT, 128], F32, kind="ExternalInput")
    dn_t = nc.dram_tensor("dn_t", [NE, 16, 128, 6, 128], F32, kind="ExternalInput")
    qkv_t = nc.dram_tensor("qkv_t", [48, 128, HKT, 128], F32, kind="ExternalInput")
    wo_t = nc.dram_tensor("wo_t", [16, 128, HKT, 128], F32, kind="ExternalInput")
    gate_t = nc.dram_tensor("gate_t", [128, HKT, NE], F32, kind="ExternalInput")
    simm = nc.dram_tensor("simm", [NE, NE], F32, kind="ExternalInput")
    cvals = nc.dram_tensor("cvals", [128, 16], F32, kind="ExternalInput")
    crow = nc.dram_tensor("crow", [1, 256], F32, kind="ExternalInput")
    iota32 = nc.dram_tensor("iota32", [128, NE], F32, kind="ExternalInput")
    ident = nc.dram_tensor("ident", [128, 128], F32, kind="ExternalInput")
    dmask = nc.dram_tensor("dmask", [128, 4 * 512], BF16, kind="ExternalInput")

    out = nc.dram_tensor("xout", [H, TPC], F32, kind="ExternalOutput")

    xc = nc.dram_tensor("xc", [H, TPC], F32)
    cc_in_q = nc.dram_tensor("cc_in_q", [NH * B * 128, THALF], BF16)
    cc_in_k = nc.dram_tensor("cc_in_k", [NH * B * 128, THALF], BF16)
    cc_in_v = nc.dram_tensor("cc_in_v", [NH * B * 128, 8 * 128], BF16)
    cc_in_a = nc.dram_tensor("cc_in_a", [4 * 128, S], BF16)
    q_all = nc.dram_tensor("q_all", [NCORES * NH * B * 128, THALF], BF16, addr_space="Shared")
    k_all = nc.dram_tensor("k_all", [NCORES * NH * B * 128, THALF], BF16, addr_space="Shared")
    v_all = nc.dram_tensor("v_all", [NCORES * NH * B * 128, 8 * 128], BF16, addr_space="Shared")
    a_all = nc.dram_tensor("a_all", [NCORES * 4 * 128, S], BF16, addr_space="Shared")

    dbg = {}
    if "ln" in dev_dump:
        dbg["ln"] = nc.dram_tensor("dbg_ln", [H, TPC], BF16, kind="ExternalOutput")
    if "combine" in dev_dump:
        dbg["combine"] = nc.dram_tensor("dbg_combine", [NE, TPC], F32, kind="ExternalOutput")
    if "qkv" in dev_dump:
        dbg["qkv"] = nc.dram_tensor("dbg_qkv", [NH * B * 128, THALF], BF16, kind="ExternalOutput")
        dbg["qkv_k"] = nc.dram_tensor("dbg_qkv_k", [NH * B * 128, THALF], BF16, kind="ExternalOutput")
        dbg["qkv_v"] = nc.dram_tensor("dbg_qkv_v", [NH * B * 128, 8 * 128], BF16, kind="ExternalOutput")
    if "attn" in dev_dump:
        dbg["attn"] = nc.dram_tensor("dbg_attn", [4 * 128, S], BF16, kind="ExternalOutput")
    if "rt" in dev_dump:
        dbg["rt_probs"] = nc.dram_tensor("dbg_rt_probs", [128, NE], F32, kind="ExternalOutput")
        dbg["rt_vals"] = nc.dram_tensor("dbg_rt_vals", [128, 8], F32, kind="ExternalOutput")
        dbg["rt_idx"] = nc.dram_tensor("dbg_rt_idx", [128, 8], U32, kind="ExternalOutput")
        dbg["rt_rw"] = nc.dram_tensor("dbg_rt_rw", [128, 8], F32, kind="ExternalOutput")
        dbg["rt_acc0"] = nc.dram_tensor("dbg_rt_acc0", [128, NE], F32, kind="ExternalOutput")
        dbg["rt_secT"] = nc.dram_tensor("dbg_rt_secT", [NE, 128], F32, kind="ExternalOutput")
        dbg["rt_cand"] = nc.dram_tensor("dbg_rt_cand", [128, NE], F32, kind="ExternalOutput")

    with ExitStack() as ctx:
        e = ctx.enter_context

        sems = {}
        for name in ("sy", "gp", "pe", "ac", "ve", "cc", "bar"):
            sems[name] = e(nc.semaphore("s_" + name))
        S_ = Sched(sems)


        A_ln = e(nc.sbuf_tensor("A_ln", [128, HKT * TPC], BF16))
        A_x = e(nc.sbuf_tensor("A_x", [128, HKT * THALF], F32))
        A_h = e(nc.sbuf_tensor("A_h", [128, 2 * 6 * THALF], BF16))
        A_wg = e(nc.sbuf_tensor("A_wg", [128, 4 * HKT * 128], BF16))
        A_wd = e(nc.sbuf_tensor("A_wd", [128, 4 * 6 * 128], BF16))
        cwb = e(nc.sbuf_tensor("cwb", [128, 2 * THALF], F32))
        combineT = e(nc.sbuf_tensor("combineT", [NE, TPC], F32))
        gate_bf = e(nc.sbuf_tensor("gate_bf", [128, HKT * NE], BF16))
        sim_sb = e(nc.sbuf_tensor("sim_sb", [NE, NE], F32))
        cpool = e(nc.sbuf_tensor("cpool", [128, 16], F32))
        cprow = e(nc.sbuf_tensor("cprow", [1, 256], F32))
        iota_sb = e(nc.sbuf_tensor("iota_sb", [128, NE], F32))
        id_f = e(nc.sbuf_tensor("id_f", [128, 128], F32))
        id_b = e(nc.sbuf_tensor("id_b", [128, 128], BF16))
        pid_sb = e(nc.sbuf_tensor("pid_sb", [1, 1], U32))

        sq_sb = e(nc.sbuf_tensor("sq_sb", [128, 2 * 512], F32))
        ln_tmp = e(nc.sbuf_tensor("ln_tmp", [128, 512], F32))

        rt_probs = e(nc.sbuf_tensor("rt_probs", [128, NE], F32))
        rt_vals = e(nc.sbuf_tensor("rt_vals", [128, 8], F32))
        rt_idx = e(nc.sbuf_tensor("rt_idx", [128, 8], U32))
        rt_idxf = e(nc.sbuf_tensor("rt_idxf", [128, 8], F32))
        rt_rw = e(nc.sbuf_tensor("rt_rw", [128, 8], F32))
        rt_oh = e(nc.sbuf_tensor("rt_oh", [128, 8 * NE], F32))
        rt_selb = e(nc.sbuf_tensor("rt_selb", [128, 8 * NE], F32))
        rt_prim = e(nc.sbuf_tensor("rt_prim", [128, NE], F32))
        rt_secT = e(nc.sbuf_tensor("rt_secT", [NE, 128], F32))
        rt_cand = e(nc.sbuf_tensor("rt_cand", [128, NE], F32))
        rt_cm8 = e(nc.sbuf_tensor("rt_cm8", [128, 8], F32))
        rt_ci8 = e(nc.sbuf_tensor("rt_ci8", [128, 8], U32))
        rt_cif = e(nc.sbuf_tensor("rt_cif", [128, 1], F32))
        rt_ohb = e(nc.sbuf_tensor("rt_ohb", [128, NE], F32))
        rt_stay = e(nc.sbuf_tensor("rt_stay", [128, 1], U32))
        rt_sel = e(nc.sbuf_tensor("rt_sel", [128, NE], F32))
        rt_acc = e(nc.sbuf_tensor("rt_acc", [128, NE], F32))
        rt_t1 = e(nc.sbuf_tensor("rt_t1", [128, 1], F32))
        rt_rsum = e(nc.sbuf_tensor("rt_rsum", [128, 1], F32))

        sg_sb = e(nc.sbuf_tensor("sg_sb", [128, 2 * 512], BF16))
        us_sb = e(nc.sbuf_tensor("us_sb", [128, 2 * 512], BF16))

        def _addr(h):
            return nc.lookup_mls(h).memorylocations[0].addr
        ax_a, ah_a, awg_a = _addr(A_x), _addr(A_h), _addr(A_wg)
        awd_a, acw_a = _addr(A_wd), _addr(cwb)
        asg_a, aus_a, asq_a = _addr(sg_sb), _addr(us_sb), _addr(sq_sb)
        stat_sb = nc.alloc_sbuf_tensor_at("stat_sb", [1, 4 * 512], F32, offset=awg_a)
        qbuf = [nc.alloc_sbuf_tensor_at("qbuf0", [128, S], BF16, offset=ax_a),
                nc.alloc_sbuf_tensor_at("qbuf1", [128, S], BF16, offset=ax_a + 16384)]
        kbuf = [nc.alloc_sbuf_tensor_at("kbuf0", [128, S], BF16, offset=ax_a + 32768),
                nc.alloc_sbuf_tensor_at("kbuf1", [128, S], BF16, offset=ax_a + 49152)]
        vbuf = [nc.alloc_sbuf_tensor_at("vbuf0", [128, S], BF16, offset=ah_a),
                nc.alloc_sbuf_tensor_at("vbuf1", [128, S], BF16, offset=awg_a)]
        strip = nc.alloc_sbuf_tensor_at("strip", [1, THALF], F32, offset=asq_a)
        dmask_sb = nc.alloc_sbuf_tensor_at("dmask_sb", [128, 4 * 512], BF16, offset=asq_a)
        denr_sb = nc.alloc_sbuf_tensor_at("denr_sb", [1, 512], F32, offset=_addr(ln_tmp))
        qk_st = nc.alloc_sbuf_tensor_at("qk_st", [128, 2 * 512], BF16, offset=asg_a)
        vt_st = nc.alloc_sbuf_tensor_at("vt_st", [128, 2 * 512], BF16, offset=aus_a)
        e_sb = nc.alloc_sbuf_tensor_at("e_sb", [128, 2 * 512], BF16, offset=acw_a)
        den_sb = nc.alloc_sbuf_tensor_at("den_sb", [128, 2 * 512], F32, offset=acw_a + 2048)
        at_st = nc.alloc_sbuf_tensor_at("at_st", [128, 2 * 512], BF16, offset=acw_a + 6144)
        o_sb = nc.alloc_sbuf_tensor_at("o_sb", [128, 2 * 512], F32, offset=awd_a)
        A_wo = nc.alloc_sbuf_tensor_at("A_wo", [128, 2 * HKT * 128], BF16, offset=acw_a)
        xt_sb = nc.alloc_sbuf_tensor_at("xt_sb", [128, 2 * 512], F32, offset=awd_a)

        pb = [e(nc.psum_tensor("pb%d" % i, [128, 512], F32)) for i in range(8)]

        CONSTS = [1.0, 0.5, 0.0, -1.0, 1.0 / H, EPS, -CEXP, SCALE, 2.0, -0.5, 0.25, 0.125, 3.0, 4.0, 5.0, 6.0]
        for ci, cv in enumerate(CONSTS):
            nc.const_aps.register(F32, cv, cpool[:, ci:ci + 1])

        S_.dma("sync", "sy", pid_sb[:, :], nc.partition_id_tensor.ap())
        S_.dma("sync", "sy", cpool[:, :], cvals[:, :])
        S_.dma("sync", "sy", cprow[:, :], crow[:, :])
        S_.dma("sync", "sy", iota_sb[:, :], iota32[:, :])
        S_.dma("sync", "sy", id_f[:, :], ident[:, :])
        t_set = S_.dma("sync", "sy", sim_sb[:, :], simm[:, :])
        S_.dma("gpsimd", "gp", gate_bf[:, :].rearrange("p (k n) -> p k n", k=HKT),
               gate_t.ap())
        t_idb = S_.dma("gpsimd", "gp", id_b[:, :], ident[:, :])
        S_.wait("sync", "sy", t_set)

        val = {}

        def _setup_regs(sy):
            val["pid"] = nc.values_load(pid_sb[:1, :1], engines=[mybir.EngineType.SP],
                                        min_val=0, max_val=NCORES - 1,
                                        skip_runtime_bounds_check=True)
        S_.run("sync", _setup_regs)
        S_.barrier()

        def emit_ln(src):
            ones_col = cpool[:, 0:1]
            ones_row = cprow[0:1, 128:256]
            ps_sx = pb[0][0:1, :]
            ps_sxx = pb[1][0:1, :]
            ps_mb = pb[2]
            ps_rb = pb[3]
            t_sq = {}
            t_sxx = {}
            t_norm_last = {}
            t_stats_done = {}
            for tt in range(NTT):
                xst = A_x[:, (tt % 2) * 8192:(tt % 2) * 8192 + 8192]
                if tt >= 2:
                    S_.wait("sync", "ve", t_norm_last[tt - 2])
                t_x = {}
                for kt in range(HKT):
                    t_x[kt] = S_.dma("sync", "sy", xst[:, kt * 512:(kt + 1) * 512],
                                     src.ap()[kt * 128:(kt + 1) * 128, tt * 512:(tt + 1) * 512])
                if tt >= 1:
                    S_.wait("tensor", "ve", t_stats_done[tt - 1])
                for kt in range(HKT):
                    xt = xst[:, kt * 512:(kt + 1) * 512]
                    sq = sq_sb[:, (kt % 2) * 512:(kt % 2) * 512 + 512]
                    S_.wait("scalar", "sy", t_x[kt])
                    if (tt, kt - 2) in t_sxx:
                        S_.wait("scalar", "pe", t_sxx[(tt, kt - 2)])
                    elif tt > 0 and kt < 2:
                        S_.wait("scalar", "pe", t_sxx[(tt - 1, 14 + kt)])
                    t_sq[(tt, kt)] = S_.compute("scalar", "ac", lambda sc, o=sq, i=xt: sc.activation(o, i, AF.Square))
                    S_.wait("tensor", "sy", t_x[kt])
                    S_.run("tensor", lambda t, o=ps_sx, r=xt, st=(kt == 0), sp=(kt == HKT - 1):
                           t.matmul(o, ones_col, r, start=st, stop=sp))
                    S_.wait("tensor", "ac", t_sq[(tt, kt)])
                    t_sxx[(tt, kt)] = S_.compute("tensor", "pe", lambda t, o=ps_sxx, r=sq, st=(kt == 0), sp=(kt == HKT - 1):
                                                 t.matmul(o, ones_col, r, start=st, stop=sp))
                m_r = stat_sb[0:1, 0:512]
                var_r = stat_sb[0:1, 512:1024]
                rstd_r = stat_sb[0:1, 1024:1536]
                tmp_r = stat_sb[0:1, 1536:2048]
                S_.wait("vector", "pe", t_sxx[(tt, HKT - 1)])
                S_.run("vector", lambda v: v.tensor_scalar_mul(m_r, ps_sx, 1.0 / H))
                S_.run("vector", lambda v: v.tensor_scalar_mul(tmp_r, ps_sxx, 1.0 / H))
                S_.run("vector", lambda v: v.drain())
                S_.run("vector", lambda v: v.tensor_mul(var_r, m_r, m_r))
                S_.run("vector", lambda v: v.drain())
                S_.run("vector", lambda v: v.tensor_sub(var_r, tmp_r, var_r))
                S_.run("vector", lambda v: v.drain())
                t_stats_done[tt] = S_.compute("vector", "ve", lambda v: v.tensor_scalar_add(var_r, var_r, EPS))
                S_.wait("scalar", "ve", t_stats_done[tt])
                t_sqr = S_.compute("scalar", "ac", lambda sc: sc.activation(tmp_r, var_r, AF.Sqrt))
                S_.wait("vector", "ac", t_sqr)
                S_.run("vector", lambda v: v.reciprocal(rstd_r, tmp_r))
                t_rs = S_.compute("vector", "ve", lambda v: v.drain())
                if tt >= 1:
                    S_.wait("tensor", "ve", t_norm_last[tt - 1])
                S_.wait("tensor", "ve", t_rs)
                S_.run("tensor", lambda t: t.matmul(ps_mb[:, :], ones_row, m_r, start=True, stop=True))
                t_bc = S_.compute("tensor", "pe", lambda t: t.matmul(ps_rb[:, :], ones_row, rstd_r, start=True, stop=True))
                S_.wait("vector", "pe", t_bc)
                for kt in range(HKT):
                    xt = xst[:, kt * 512:(kt + 1) * 512]
                    lnt = A_ln[:, kt * TPC + tt * 512: kt * TPC + tt * 512 + 512]
                    S_.run("vector", lambda v, x_=xt: v.tensor_sub(ln_tmp[:, :], x_, ps_mb[:, :]))
                    t_norm_last[tt] = S_.compute("vector", "ve", lambda v, o=lnt: v.tensor_mul(o, ln_tmp[:, :], ps_rb[:, :]))
            return t_norm_last[NTT - 1]

        def emit_routing():
            ps_lg = pb[4][:, 0:NE]
            ps_tr = pb[5]
            ps_sr = pb[6][:, 0:NE]
            ps_cmb = pb[7]
            t_ec = None
            for tile in range(HKT):
                t0 = tile * 128
                if t_ec is not None:
                    S_.wait("tensor", "ac", t_ec)
                    S_.wait("tensor", "ve", S_.cnt["ve"])
                for kt in range(HKT):
                    lnt = A_ln[:, kt * TPC + t0: kt * TPC + t0 + 128]
                    gt = gate_bf[:, kt * NE:(kt + 1) * NE]
                    if kt < HKT - 1:
                        S_.run("tensor", lambda t, l=lnt, r=gt, st=(kt == 0): t.matmul(ps_lg, l, r, start=st, stop=False))
                    else:
                        t_lg = S_.compute("tensor", "pe", lambda t, l=lnt, r=gt: t.matmul(ps_lg, l, r, start=False, stop=True))
                S_.wait("vector", "pe", t_lg)
                S_.run("vector", lambda v: v.tensor_reduce(rt_t1[:, :], ps_lg, axis=AX.X, op=ALU.max, negate=True))
                t_evl = S_.compute("vector", "ve", lambda v: v.tensor_copy(rt_probs[:, :], ps_lg))
                S_.wait("scalar", "ve", t_evl)
                t_exp = S_.compute("scalar", "ac", lambda sc: sc.activation(rt_probs[:, :], rt_probs[:, :], AF.Exp,
                                                                            bias=rt_t1[:, :], scale=1.0))
                S_.wait("vector", "ac", t_exp)
                S_.run("vector", lambda v: v.max(out=rt_vals[:, :], in_=rt_probs[:, :]))
                S_.run("vector", lambda v: v.drain())
                S_.run("vector", lambda v: v.max_index(rt_idx[:, :], rt_vals[:, :], rt_probs[:, :]))
                S_.run("vector", lambda v: v.drain())
                S_.run("vector", lambda v: v.tensor_copy(rt_idxf[:, :], rt_idx[:, :]))
                S_.run("vector", lambda v: v.drain())
                S_.run("vector", lambda v: v.tensor_reduce(rt_rsum[:, :], rt_vals[:, :], axis=AX.X, op=ALU.add))
                S_.run("vector", lambda v: v.drain())
                S_.run("vector", lambda v: v.reciprocal(rt_t1[:, :], rt_rsum[:, :]))
                S_.run("vector", lambda v: v.drain())
                S_.run("vector", lambda v: v.tensor_scalar(rt_rw[:, :], rt_vals[:, :], rt_t1[:, :], None, ALU.mult))
                S_.run("vector", lambda v: v.drain())
                for k in range(8):
                    S_.run("vector", lambda v, k=k: v.tensor_scalar(
                        rt_oh[:, k * NE:(k + 1) * NE], iota_sb[:, :], rt_idxf[:, k:k + 1], None, ALU.is_equal))
                for k in range(4):
                    S_.run("vector", lambda v, k=k: v.tensor_scalar(
                        rt_selb[:, k * NE:(k + 1) * NE], rt_oh[:, k * NE:(k + 1) * NE],
                        rt_rw[:, k:k + 1], None, ALU.mult))
                S_.run("vector", lambda v: v.drain())
                S_.run("vector", lambda v: v.tensor_reduce(
                    rt_prim[:, :], rt_oh[:, 0:4 * NE].rearrange("p (k e) -> p e k", k=4), axis=AX.X, op=ALU.add))
                t_oh_ready = S_.compute("vector", "ve", lambda v: v.tensor_copy(rt_cif[:, :], rt_idxf[:, 0:1]))
                if tile == 0 and "rt_probs" in dbg:
                    S_.wait("sync", "ve", t_oh_ready)
                    S_.dma("sync", "sy", dbg["rt_probs"].ap()[:, :], rt_probs[:, :])
                    S_.dma("sync", "sy", dbg["rt_vals"].ap()[:, :], rt_vals[:, :])
                    S_.dma("sync", "sy", dbg["rt_idx"].ap()[:, :], rt_idx[:, :])
                    S_.dma("sync", "sy", dbg["rt_rw"].ap()[:, :], rt_rw[:, :])
                    S_.dma("sync", "sy", dbg["rt_secT"].ap()[0:1, 0:1], rt_rsum[:1, 0:1])
                    t_dd = S_.dma("sync", "sy", dbg["rt_acc0"].ap()[:, :], rt_acc[:, :])
                    S_.wait("vector", "sy", t_dd)
                for j in range(4):
                    ohj = rt_oh[:, (4 + j) * NE:(5 + j) * NE]
                    S_.wait("tensor", "ve", t_oh_ready if j == 0 else t_addacc)
                    t_tr = S_.compute("tensor", "pe", lambda t, i=ohj: t.transpose(ps_tr[0:NE, 0:128], i, id_f[:, :]))
                    S_.wait("scalar", "pe", t_tr)
                    t_ev = S_.compute("scalar", "ac", lambda sc: sc.copy(rt_secT[:, :], ps_tr[0:NE, 0:128]))
                    S_.wait("tensor", "ac", t_ev)
                    t_sr = S_.compute("tensor", "pe", lambda t: t.matmul(ps_sr, rt_secT[:, :], sim_sb[:, :],
                                                                         start=True, stop=True))
                    S_.wait("vector", "pe", t_sr)
                    S_.run("vector", lambda v: v.tensor_mul(rt_cand[:, :], ps_sr, rt_prim[:, :]))
                    if tile == 0 and j == 0 and "rt_cand" in dbg:
                        t_dd2 = S_.compute("vector", "ve", lambda v: v.engine_nop())
                        S_.wait("sync", "ve", t_dd2)
                        S_.dma("sync", "sy", dbg["rt_secT"].ap()[:, :], rt_secT[:, :])
                        t_dd3 = S_.dma("sync", "sy", dbg["rt_cand"].ap()[:, :], rt_cand[:, :])
                        S_.wait("vector", "sy", t_dd3)
                    S_.run("vector", lambda v: v.drain())
                    S_.run("vector", lambda v: v.max(out=rt_cm8[:, :], in_=rt_cand[:, :]))
                    S_.run("vector", lambda v: v.drain())
                    S_.run("vector", lambda v: v.max_index(rt_ci8[:, :], rt_cm8[:, :], rt_cand[:, :]))
                    S_.run("vector", lambda v: v.drain())
                    S_.run("vector", lambda v: v.tensor_copy(rt_cif[:, :], rt_ci8[:, 0:1]))
                    S_.run("vector", lambda v: v.drain())
                    S_.run("vector", lambda v: v.tensor_scalar(rt_ohb[:, :], iota_sb[:, :], rt_cif[:, :], None, ALU.is_equal))
                    S_.run("vector", lambda v: v.drain())
                    S_.run("vector", lambda v: v.tensor_scalar(rt_stay[:, :], rt_cm8[:, 0:1], 0.5, None, ALU.is_lt))
                    S_.run("vector", lambda v: v.drain())
                    S_.run("vector", lambda v, oh=ohj, k=4 + j: v.select(
                        rt_selb[:, k * NE:(k + 1) * NE], rt_stay[:, 0:1].to_broadcast([128, NE]), oh, rt_ohb[:, :]))
                    S_.run("vector", lambda v: v.drain())
                    t_addacc = S_.compute("vector", "ve", lambda v, k=4 + j: v.tensor_scalar(
                        rt_selb[:, k * NE:(k + 1) * NE], rt_selb[:, k * NE:(k + 1) * NE],
                        rt_rw[:, k:k + 1], None, ALU.mult))
                S_.run("vector", lambda v: v.drain())
                t_addacc = S_.compute("vector", "ve", lambda v: v.tensor_reduce(
                    rt_acc[:, :], rt_selb[:, :].rearrange("p (k e) -> p e k", k=8), axis=AX.X, op=ALU.add))
                S_.wait("tensor", "ve", t_addacc)
                t_tc = S_.compute("tensor", "pe", lambda t: t.transpose(ps_cmb[0:NE, 0:128], rt_acc[:, :], id_f[:, :]))
                S_.wait("scalar", "pe", t_tc)
                t_ec = S_.compute("scalar", "ac", lambda sc, c0=t0: sc.copy(combineT[:, c0:c0 + 128], ps_cmb[0:NE, 0:128]))
            S_.wait("vector", "ac", t_ec)
            return t_ec

        def emit_ffn_half(T, src, dst):
            x_half = A_x
            t_xl = 0
            for g4 in range(4):
                t_xl = S_.dma("sync", "sy",
                              x_half[:, g4 * 4 * THALF:(g4 + 1) * 4 * THALF].rearrange("p (k t) -> p k t", k=4),
                              src.ap().rearrange("(kt p) t -> p kt t", p=128)[:, g4 * 4:(g4 + 1) * 4,
                                                                             T * THALF:(T + 1) * THALF])

            pg = [pb[0], pb[1]]
            pu = [pb[2], pb[3]]
            pdmap = {(0, 0): pb[4], (0, 1): pb[5], (1, 0): pb[6], (1, 1): pb[7]}
            pcw = [pb[4], pb[5]]

            half_row = cprow[0:1, 0:128]

            gu_jobs = [(ee, j) for ee in range(NE) for j in range(6)]
            gu_done = {}
            t_gmm = {}
            t_umm = {}
            t_silu = {}
            t_us = {}
            t_h = {}
            t_dmm = {}
            t_xacc = {}
            t_cwev = {}

            def gu_buf(j, gi):
                slot = (j % 2) * 2 + gi
                return A_wg[:, slot * 2048: slot * 2048 + 2048]

            def wd_buf(ee, o):
                slot = (ee % 2) * 2 + (o % 2)
                return A_wd[:, slot * 768: slot * 768 + 768]

            def emit_gu_dma(idx):
                if idx >= len(gu_jobs):
                    return
                ee, j = gu_jobs[idx]
                if idx >= 2:
                    pee, pj = gu_jobs[idx - 2]
                    S_.wait("gpsimd", "pe", t_umm[(pee, pj, 1)])
                for gi, wslot in ((0, j), (1, 6 + j)):
                    gu_done[(ee, j)] = S_.dma("gpsimd", "gp",
                                              gu_buf(j, gi).rearrange("p (k n) -> p k n", k=HKT),
                                              gu_t.ap()[ee, wslot])

            wd_jobs = [(p_, o_, ei_) for p_ in range(NE // 2) for o_ in range(16) for ei_ in range(2)]
            wd_done = {}

            def emit_wd_dma(idx):
                if idx >= len(wd_jobs):
                    return
                p_, o_, ei_ = wd_jobs[idx]
                ee = 2 * p_ + ei_
                if idx >= 4:
                    pp, po, pei = wd_jobs[idx - 4]
                    S_.wait("gpsimd", "pe", t_dmm[(pp, po)])
                wd_done[(ee, o_)] = S_.dma("gpsimd", "gp",
                                           wd_buf(ee, o_).rearrange("p (k n) -> p k n", k=6),
                                           dn_t.ap()[ee, o_])

            emit_gu_dma(0)
            emit_gu_dma(1)
            gu_pf = 2
            wd_pf = 0
            t_cwmm_prev = 0
            S_.wait("vector", "sy", t_xl)

            for p in range(NE // 2):
                pair = (2 * p, 2 * p + 1)
                for ei, ee in enumerate(pair):
                    st_buf = strip[0:1, 0:THALF]
                    if ee > 0:
                        S_.wait("sync", "pe", t_cwmm_prev)
                    t_strip = S_.dma("sync", "sy", st_buf, combineT[ee:ee + 1, T * THALF:(T + 1) * THALF])
                    S_.wait("tensor", "sy", t_strip)
                    if p > 0:
                        S_.wait("tensor", "ve", t_xacc[(p - 1, 15, 1)])
                        S_.wait("scalar", "ve", t_us[(2 * (p - 1) + ei, 5, 1)])
                    for th in range(2):
                        if th == 0:
                            S_.run("tensor", lambda t, r=st_buf[0:1, 0:512]: t.matmul(pcw[0][:, :], half_row, r, start=True, stop=True))
                        else:
                            t_cwmm = S_.compute("tensor", "pe",
                                                lambda t, r=st_buf[0:1, 512:1024]: t.matmul(pcw[1][:, :], half_row, r, start=True, stop=True))
                    t_cwmm_prev = t_cwmm
                    S_.wait("scalar", "pe", t_cwmm)
                    for th in range(2):
                        t_cwev[ee] = S_.compute("scalar", "ac",
                                                lambda sc, o=cwb[:, ei * THALF + th * 512: ei * THALF + th * 512 + 512],
                                                i=pcw[th][:, :]: sc.copy(o, i))

                for ei, ee in enumerate(pair):
                    hb = A_h[:, ei * 6 * THALF:(ei + 1) * 6 * THALF]
                    for j in range(6):
                        if p == 0 and ei == 0 and j == 0:
                            for _ in range(4):
                                emit_wd_dma(wd_pf); wd_pf += 1
                        gbuf, ubuf = gu_buf(j, 0), gu_buf(j, 1)
                        S_.wait("tensor", "gp", gu_done[(ee, j)])
                        for gi, wb_, pgu, tdict in ((0, gbuf, pg, t_gmm), (1, ubuf, pu, t_umm)):
                            for th in range(2):
                                prev = (ee, j - 1, th) if j >= 1 else ((ee - 1, 5, th) if ee > 0 else None)
                                if prev is not None and prev in t_silu:
                                    if gi == 0:
                                        S_.wait("tensor", "ac", t_silu[prev])
                                    else:
                                        S_.wait("tensor", "ve", t_us[prev])
                                rhs0 = kt_rhs = None
                                for kt in range(HKT):
                                    rhs = A_ln[:, kt * TPC + T * THALF + th * 512: kt * TPC + T * THALF + th * 512 + 512]
                                    if kt < HKT - 1:
                                        S_.run("tensor", lambda t, o=pgu[th][:, :], l=wb_[:, kt * 128:(kt + 1) * 128], r=rhs,
                                               st=(kt == 0): t.matmul(o, l, r, start=st, stop=False))
                                    else:
                                        tdict[(ee, j, th)] = S_.compute(
                                            "tensor", "pe",
                                            lambda t, o=pgu[th][:, :], l=wb_[:, kt * 128:(kt + 1) * 128], r=rhs:
                                            t.matmul(o, l, r, start=False, stop=True))
                        if j < 5:
                            emit_gu_dma(gu_pf); gu_pf += 1
                        for th in range(2):
                            S_.wait("scalar", "pe", t_gmm[(ee, j, th)])
                            hprev = (ee, j - 1, th) if j >= 1 else ((ee - 1, 5, th) if ee > 0 else None)
                            if hprev is not None and hprev in t_h:
                                S_.wait("scalar", "ve", t_h[hprev])
                            t_silu[(ee, j, th)] = S_.compute("scalar", "ac",
                                                             lambda sc, o=sg_sb[:, th * 512:(th + 1) * 512], i=pg[th][:, :]:
                                                             sc.activation(o, i, AF.Silu))
                            S_.wait("vector", "pe", t_umm[(ee, j, th)])
                            S_.wait("vector", "ac", t_cwev[ee])
                            t_us[(ee, j, th)] = S_.compute("vector", "ve",
                                                           lambda v, o=us_sb[:, th * 512:(th + 1) * 512], i=pu[th][:, :],
                                                           c=cwb[:, ei * THALF + th * 512: ei * THALF + th * 512 + 512]:
                                                           v.scalar_tensor_tensor(o, i, 1.0, c, ALU.mult, ALU.mult))
                            S_.wait("vector", "ac", t_silu[(ee, j, th)])
                            t_h[(ee, j, th)] = S_.compute("vector", "ve",
                                                          lambda v, o=hb[:, j * THALF + th * 512: j * THALF + th * 512 + 512],
                                                          a=us_sb[:, th * 512:(th + 1) * 512],
                                                          b=sg_sb[:, th * 512:(th + 1) * 512]:
                                                          v.tensor_mul(o, a, b))
                    emit_gu_dma(gu_pf); gu_pf += 1
                S_.wait("tensor", "ve", t_h[(pair[1], 5, 1)])
                S_.wait("tensor", "ac", t_cwev[pair[1]])
                for o in range(16):
                    for ei, ee in enumerate(pair):
                        S_.wait("tensor", "gp", wd_done[(ee, o)])
                        if ei == 0:
                            if o >= 2:
                                S_.wait("tensor", "ve", t_xacc[(p, o - 2, 1)])
                            elif p > 0:
                                S_.wait("tensor", "ve", t_xacc[(p - 1, 14 + o, 1)])
                        wbuf = wd_buf(ee, o)
                        hb = A_h[:, ei * 6 * THALF:(ei + 1) * 6 * THALF]
                        for kt in range(6):
                            for th in range(2):
                                lm = (ei == 1 and kt == 5 and th == 1)
                                if not lm:
                                    S_.run("tensor", lambda t, o_=pdmap[(o % 2, th)][:, :],
                                           l=wbuf[:, kt * 128:(kt + 1) * 128],
                                           r=hb[:, kt * THALF + th * 512: kt * THALF + th * 512 + 512],
                                           st=(ei == 0 and kt == 0): t.matmul(o_, l, r, start=st, stop=False))
                                else:
                                    t_dmm[(p, o)] = S_.compute("tensor", "pe",
                                                               lambda t, o_=pdmap[(o % 2, th)][:, :],
                                                               l=wbuf[:, kt * 128:(kt + 1) * 128],
                                                               r=hb[:, kt * THALF + th * 512: kt * THALF + th * 512 + 512]:
                                                               t.matmul(o_, l, r, start=False, stop=True))
                    S_.wait("vector", "pe", t_dmm[(p, o)])
                    for th in range(2):
                        t_xacc[(p, o, th)] = S_.compute(
                            "vector", "ve",
                            lambda v, xo=x_half[:, o * THALF + th * 512: o * THALF + th * 512 + 512],
                            ps=pdmap[(o % 2, th)][:, :]: v.tensor_add(xo, xo, ps))
                    emit_wd_dma(wd_pf); wd_pf += 1
                    emit_wd_dma(wd_pf); wd_pf += 1
            S_.wait("sync", "ve", t_xacc[(NE // 2 - 1, 15, 1)])
            t_xs = 0
            for g4 in range(4):
                t_xs = S_.dma("sync", "sy",
                              dst.ap().rearrange("(kt p) t -> p kt t", p=128)[:, g4 * 4:(g4 + 1) * 4,
                                                                             T * THALF:(T + 1) * THALF],
                              x_half[:, g4 * 4 * THALF:(g4 + 1) * 4 * THALF].rearrange("p (k t) -> p k t", k=4))
            S_.wait("sync", "sy", t_xs)
            return

        def emit_qkv():
            pq = [pb[0], pb[1]]
            ptr_b = pb[2].ap().bitcast(BF16)
            t_wdma = {}
            t_mm = {}
            t_ev = {}
            t_vtr = {}
            t_vev = {}
            t_st = {}

            def wbuf(o):
                return A_wg[:, (o % 2) * 2048:(o % 2) * 2048 + 2048]

            def emit_wdma(o):
                if o >= 48:
                    return
                if o >= 2:
                    S_.wait("gpsimd", "pe", t_mm[(o - 2, NTT - 1)])
                t_wdma[o] = S_.dma("gpsimd", "gp", wbuf(o).rearrange("p (k n) -> p k n", k=HKT),
                                   qkv_t.ap()[o])

            emit_wdma(0)
            emit_wdma(1)
            for o in range(48):
                typ, head = o // 16, o % 16
                S_.wait("tensor", "gp", t_wdma[o])
                for tt in range(NTT):
                    bb, tof = tt // 2, (tt % 2) * 512
                    if (o, tt - 2) in t_ev:
                        S_.wait("tensor", "ac", t_ev[(o, tt - 2)])
                    elif o > 0 and tt < 2:
                        S_.wait("tensor", "ac", t_ev[(o - 1, 2 + tt)])
                    for kt in range(HKT):
                        rhs = A_ln[:, kt * TPC + tt * 512: kt * TPC + tt * 512 + 512]
                        if kt < HKT - 1:
                            S_.run("tensor", lambda t, l=wbuf(o)[:, kt * 128:(kt + 1) * 128], r=rhs,
                                   o_=pq[tt % 2][:, :], st=(kt == 0): t.matmul(o_, l, r, start=st, stop=False))
                        else:
                            t_mm[(o, tt)] = S_.compute("tensor", "pe",
                                                       lambda t, l=wbuf(o)[:, kt * 128:(kt + 1) * 128], r=rhs,
                                                       o_=pq[tt % 2][:, :]: t.matmul(o_, l, r, start=False, stop=True))
                    S_.wait("scalar", "pe", t_mm[(o, tt)])
                    if (o, tt - 2) in t_st:
                        S_.wait("scalar", "sy", t_st[(o, tt - 2)])
                    elif o > 0 and tt < 2:
                        S_.wait("scalar", "sy", t_st[(o - 1, 2 + tt)])
                    t_ev[(o, tt)] = S_.compute("scalar", "ac",
                                               lambda sc, o_=qk_st[:, (tt % 2) * 512:(tt % 2) * 512 + 512],
                                               i=pq[tt % 2][:, :]: sc.copy(o_, i))
                    if typ < 2:
                        dst = cc_in_q if typ == 0 else cc_in_k
                        S_.wait("sync", "ac", t_ev[(o, tt)])
                        t_st[(o, tt)] = S_.dma("sync", "sy",
                                               dst.ap()[(head * 2 + bb) * 128:(head * 2 + bb) * 128 + 128,
                                                        tof:tof + 512],
                                               qk_st[:, (tt % 2) * 512:(tt % 2) * 512 + 512])
                    else:
                        S_.wait("tensor", "ac", t_ev[(o, tt)])
                        if (o, tt - 2) in t_vev:
                            S_.wait("tensor", "ac", t_vev[(o, tt - 2)])
                        elif o > 32 and tt < 2:
                            S_.wait("tensor", "ac", t_vev[(o - 1, 2 + tt)])
                        for tb in range(4):
                            t_vtr[(o, tt)] = S_.compute("tensor", "pe",
                                                        lambda t, i=qk_st[:, (tt % 2) * 512 + tb * 128:
                                                                          (tt % 2) * 512 + tb * 128 + 128],
                                                        o_=ptr_b[:, tb * 128:(tb + 1) * 128]:
                                                        t.transpose(o_, i, id_b[:, :]))
                        S_.wait("scalar", "pe", t_vtr[(o, tt)])
                        if (o, tt - 2) in t_st:
                            S_.wait("scalar", "sy", t_st[(o, tt - 2)])
                        elif o > 32 and tt < 2:
                            S_.wait("scalar", "sy", t_st[(o - 1, 2 + tt)])
                        t_vev[(o, tt)] = S_.compute("scalar", "ac",
                                                    lambda sc, o_=vt_st[:, (tt % 2) * 512:(tt % 2) * 512 + 512],
                                                    i=ptr_b[:, 0:512]: sc.copy(o_, i))
                        S_.wait("sync", "ac", t_vev[(o, tt)])
                        for tb in range(4):
                            ttg = (tt % 2) * 4 + tb
                            t_st[(o, tt)] = S_.dma("sync", "sy",
                                                   cc_in_v.ap()[(head * 2 + bb) * 128:
                                                                (head * 2 + bb) * 128 + 128,
                                                                ttg * 128:(ttg + 1) * 128],
                                                   vt_st[:, (tt % 2) * 512 + tb * 128:(tt % 2) * 512 + tb * 128 + 128])
                emit_wdma(o + 2)
            S_.wait("sync", "sy", S_.cnt["sy"])
            return

        def emit_ag(pairs):
            n = 0
            for src_t, dst_t in pairs:
                S_.compute("gpsimd", "cc",
                           lambda g, s=src_t, d=dst_t: g.collective_compute(
                               "AllGather", ALU.bypass, replica_groups=[list(range(NCORES))],
                               ins=[s.ap().opt()], outs=[d.ap().opt()]))
                n += 1
            S_.wait("gpsimd", "cc", S_.cnt["cc"])
            return

        def emit_attention():
            t_dm = S_.dma("sync", "sy", dmask_sb[:, :], dmask.ap())
            pS = [pb[0], pb[1]]
            pO = [pb[2], pb[3]]
            pDen = pb[4]
            pB = pb[5]
            t_ld = {}
            t_exp = {}
            t_den = {}
            t_pv = {}
            t_maskv = {}
            t_scl = {}
            t_ost = {}
            t_rec = {}
            t_dred = {}
            last_rec = 0
            last_bc = 0
            last_scl = 0
            last_dred = {}

            def emit_job_loads(j):
                if j >= 4:
                    return
                jb = j % 2
                bb, hh = j % 2, j // 2
                if j >= 2:
                    S_.wait("sync", "pe", t_pv[(j - 2, 15)])
                def _dyndma(sy, off_fn, src_t, patt, dst_ap):
                    sy.dma_start(out=dst_ap, in_=bass.AP(src_t, off_fn(), patt)).then_inc(sems["sy"], 16)
                for tens, buf in ((q_all, qbuf[jb]), (k_all, kbuf[jb])):
                    dst = buf[:, :].rearrange("p (s t) -> p s t", s=NCORES)
                    S_.inc("sy", 16)
                    S_.run("sync", lambda sy, jj=j, s=tens, d=dst:
                           _dyndma(sy, lambda: (val["pid"] * 4 + jj) * (128 * THALF),
                                   s, [[THALF, 128], [4096 * THALF, NCORES], [1, THALF]], d))
                dstv = vbuf[jb][:, :].rearrange("p (s td) -> p s td", s=NCORES)
                S_.inc("sy", 16)
                S_.run("sync", lambda sy, jj=j, d=dstv:
                       _dyndma(sy, lambda: (val["pid"] * 4 + (jj // 2) * 2 + (jj % 2)) * (128 * 1024),
                               v_all, [[1024, 128], [4096 * 1024, NCORES], [1, 1024]], d))
                t_ld[j] = S_.cnt["sy"]

            emit_job_loads(0)
            emit_job_loads(1)
            S_.wait("vector", "sy", t_dm)
            ikb = 0
            for j in range(4):
                jb = j % 2
                S_.wait("tensor", "sy", t_ld[j])
                for qt in range(16):
                    nkb = 4 * qt + 4
                    qsl = qbuf[jb][:, qt * 512:(qt + 1) * 512]
                    if (j, qt - 2) in t_scl:
                        S_.wait("tensor", "ve", t_scl[(j, qt - 2)])
                    elif j > 0 and qt < 2:
                        S_.wait("tensor", "ve", t_scl[(j - 1, 14 + qt)])
                    for kb in range(nkb):
                        par = ikb % 2
                        if ikb >= 2:
                            S_.wait("tensor", "ac", t_exp[ikb - 2])
                        ksl = kbuf[jb][:, kb * 128:(kb + 1) * 128]
                        vsl = vbuf[jb][:, kb * 128:(kb + 1) * 128]
                        t_qk = S_.compute("tensor", "pe",
                                          lambda t, l=ksl, r=qsl, o_=pS[par][:, :]:
                                          t.matmul(o_, l, r, start=True, stop=True))
                        r = kb - 4 * qt
                        if r >= 0:
                            S_.wait("vector", "pe", t_qk)
                            t_maskv[ikb] = S_.compute("vector", "ve",
                                                      lambda v, o_=pS[par][:, :],
                                                      m=dmask_sb[:, r * 512:(r + 1) * 512]:
                                                      v.tensor_add(o_, o_, m))
                            S_.wait("scalar", "ve", t_maskv[ikb])
                        else:
                            S_.wait("scalar", "pe", t_qk)
                        if ikb >= 2:
                            S_.wait("scalar", "pe", t_pv[ikb - 2] if (ikb - 2) in t_pv else 0)
                            S_.wait("scalar", "ve", t_den[ikb - 2])
                        t_exp[ikb] = S_.compute("scalar", "ac",
                                                lambda sc, i=pS[par][:, :],
                                                o_=e_sb[:, par * 512:(par + 1) * 512]:
                                                sc.activation(o_, i, AF.Exp, bias=cpool[:, 6:7], scale=SCALE))
                        S_.wait("vector", "ac", t_exp[ikb])
                        if kb == 0:
                            S_.wait("vector", "pe", last_dred.get(qt % 2, 0))
                            S_.run("vector", lambda v, d=den_sb[:, (qt % 2) * 512:(qt % 2) * 512 + 512]:
                                   v.memset(d, 0.0))
                        t_den[ikb] = S_.compute("vector", "ve",
                                                lambda v, d=den_sb[:, (qt % 2) * 512:(qt % 2) * 512 + 512],
                                                s=e_sb[:, par * 512:(par + 1) * 512]:
                                                v.tensor_add(d, d, s))
                        S_.wait("tensor", "ac", t_exp[ikb])
                        t_pv[(j, qt)] = t_pv[ikb] = S_.compute("tensor", "pe",
                                                               lambda t, l=vsl,
                                                               r=e_sb[:, par * 512:(par + 1) * 512],
                                                               o_=pO[qt % 2][:, :], st=(kb == 0):
                                                               t.matmul(o_, l, r, start=st, stop=(kb == nkb - 1)))
                        ikb += 1
                    S_.wait("tensor", "ve", last_rec)
                    S_.wait("tensor", "ve", t_den[ikb - 1])
                    t_dred[(j, qt)] = last_dred[qt % 2] = S_.compute(
                        "tensor", "pe",
                        lambda t, r=den_sb[:, (qt % 2) * 512:(qt % 2) * 512 + 512]:
                        t.matmul(pDen[0:1, :], cpool[:, 0:1], r, start=True, stop=True))
                    S_.wait("vector", "pe", t_dred[(j, qt)])
                    S_.wait("vector", "pe", last_bc)
                    S_.run("vector", lambda v: v.reciprocal(denr_sb[0:1, :], pDen[0:1, :]))
                    t_rec[(j, qt)] = last_rec = S_.compute("vector", "ve", lambda v: v.drain())
                    S_.wait("tensor", "ve", t_rec[(j, qt)])
                    S_.wait("tensor", "ve", last_scl)
                    t_bc = last_bc = S_.compute("tensor", "pe",
                                      lambda t: t.matmul(pB[:, :], cprow[0:1, 128:256],
                                                         denr_sb[0:1, :], start=True, stop=True))
                    S_.wait("scalar", "pe", t_pv[(j, qt)])
                    if (j, qt - 2) in t_ost:
                        S_.wait("scalar", "sy", t_ost[(j, qt - 2)])
                    elif j > 0 and qt < 2:
                        S_.wait("scalar", "sy", t_ost[(j - 1, 14 + qt)])
                    t_oev = S_.compute("scalar", "ac",
                                       lambda sc, i=pO[qt % 2][:, :],
                                       o_=o_sb[:, (qt % 2) * 512:(qt % 2) * 512 + 512]: sc.copy(o_, i))
                    S_.wait("vector", "ac", t_oev)
                    S_.wait("vector", "pe", t_bc)
                    t_scl[(j, qt)] = last_scl = S_.compute("vector", "ve",
                                                lambda v, a=o_sb[:, (qt % 2) * 512:(qt % 2) * 512 + 512],
                                                o_=at_st[:, (qt % 2) * 512:(qt % 2) * 512 + 512]:
                                                v.tensor_mul(o_, a, pB[:, :]))
                    S_.wait("sync", "ve", t_scl[(j, qt)])
                    t_ost[(j, qt)] = S_.dma("sync", "sy",
                                            cc_in_a.ap()[j * 128:(j + 1) * 128, qt * 512:(qt + 1) * 512],
                                            at_st[:, (qt % 2) * 512:(qt % 2) * 512 + 512])
                emit_job_loads(j + 2)
            S_.wait("sync", "sy", S_.cnt["sy"])
            return

        def emit_wo():
            def _xdma(sy):
                dst = A_ln[:, :].rearrange("p (g t) -> p g t", g=NH * B)
                sy.dma_start(out=dst, in_=bass.AP(a_all, val["pid"] * THALF,
                                                  [[S, 128], [128 * S, NH * B], [1, THALF]])).then_inc(sems["sy"], 16)
            S_.inc("sy", 16)
            S_.run("sync", _xdma)
            t_ex = S_.cnt["sy"]
            S_.wait("tensor", "sy", t_ex)

            pw = [pb[0], pb[1]]
            t_wdma = {}
            t_mm = {}
            t_add = {}
            t_xl = {}
            t_xs = {}

            def wobuf(o):
                return A_wo[:, (o % 2) * 2048:(o % 2) * 2048 + 2048]

            def emit_wodma(o):
                if o >= 16:
                    return
                if o >= 2:
                    S_.wait("gpsimd", "pe", t_mm[(o - 2, NTT - 1)])
                t_wdma[o] = S_.dma("gpsimd", "gp", wobuf(o).rearrange("p (k n) -> p k n", k=HKT),
                                   wo_t.ap()[o])

            emit_wodma(0)
            emit_wodma(1)
            for o in range(16):
                S_.wait("tensor", "gp", t_wdma[o])
                for tt in range(NTT):
                    if (o, tt) not in t_xl:
                        pass
                    if (o, tt - 2) in t_xs:
                        S_.wait("sync", "sy", t_xs[(o, tt - 2)])
                    elif o > 0 and tt < 2:
                        S_.wait("sync", "sy", t_xs[(o - 1, 2 + tt)])
                    t_xl[(o, tt)] = S_.dma("sync", "sy", xt_sb[:, (tt % 2) * 512:(tt % 2) * 512 + 512],
                                           xc.ap()[o * 128:(o + 1) * 128, tt * 512:(tt + 1) * 512])
                    if (o, tt - 2) in t_add:
                        S_.wait("tensor", "ve", t_add[(o, tt - 2)])
                    elif o > 0 and tt < 2:
                        S_.wait("tensor", "ve", t_add[(o - 1, 2 + tt)])
                    for kt in range(HKT):
                        rhs = A_ln[:, kt * TPC + tt * 512: kt * TPC + tt * 512 + 512]
                        if kt < HKT - 1:
                            S_.run("tensor", lambda t, l=wobuf(o)[:, kt * 128:(kt + 1) * 128], r=rhs,
                                   o_=pw[tt % 2][:, :], st=(kt == 0): t.matmul(o_, l, r, start=st, stop=False))
                        else:
                            t_mm[(o, tt)] = S_.compute("tensor", "pe",
                                                       lambda t, l=wobuf(o)[:, kt * 128:(kt + 1) * 128], r=rhs,
                                                       o_=pw[tt % 2][:, :]: t.matmul(o_, l, r, start=False, stop=True))
                    S_.wait("vector", "pe", t_mm[(o, tt)])
                    S_.wait("vector", "sy", t_xl[(o, tt)])
                    t_add[(o, tt)] = S_.compute("vector", "ve",
                                                lambda v, x_=xt_sb[:, (tt % 2) * 512:(tt % 2) * 512 + 512],
                                                p_=pw[tt % 2][:, :]: v.tensor_add(x_, x_, p_))
                    S_.wait("sync", "ve", t_add[(o, tt)])
                    t_xs[(o, tt)] = S_.dma("sync", "sy", xc.ap()[o * 128:(o + 1) * 128, tt * 512:(tt + 1) * 512],
                                           xt_sb[:, (tt % 2) * 512:(tt % 2) * 512 + 512])
                emit_wodma(o + 2)
            S_.wait("sync", "sy", S_.cnt["sy"])
            return

        if stop_after == "attn_only":
            emit_attention()
            stop_after = "done"
        if stop_after == "done":
            pass
        elif True:
            emit_ln(x0)
        S_.barrier()
        if stop_after == "done":
            pass
        elif "ln" in dbg:
            t_d = 0
            for kt in range(HKT):
                t_d = S_.dma("sync", "sy", dbg["ln"].ap()[kt * 128:(kt + 1) * 128, :], A_ln[:, kt * TPC:(kt + 1) * TPC])
            S_.wait("sync", "sy", t_d)
            S_.barrier()
        if stop_after == "done":
            pass
        elif stop_after != "ln1":
            emit_routing()
            S_.barrier()
            if "combine" in dbg:
                t_d = S_.dma("sync", "sy", dbg["combine"].ap()[:, :], combineT[:, :])
                S_.wait("sync", "sy", t_d)
                S_.barrier()
            if stop_after != "routing1":
                emit_ffn_half(0, x0, xc)
                S_.barrier()
                emit_ffn_half(1, x0, xc)
                S_.barrier()

                def _copy_xc_out():
                    t_o = 0
                    for g4 in range(4):
                        t_o = S_.dma("sync", "sy",
                                     out.ap().rearrange("(kt p) t -> p kt t", p=128)[:, g4 * 4:(g4 + 1) * 4, :],
                                     xc.ap().rearrange("(kt p) t -> p kt t", p=128)[:, g4 * 4:(g4 + 1) * 4, :])
                    S_.wait("sync", "sy", t_o)

                if stop_after == "ffn1":
                    _copy_xc_out()
                else:
                    emit_ln(xc)
                    S_.barrier()
                    emit_qkv()
                    S_.barrier()
                    if "qkv" in dbg:
                        S_.dma("sync", "sy", dbg["qkv"].ap()[:, :], cc_in_q.ap()[:, :])
                        S_.dma("sync", "sy", dbg["qkv_k"].ap()[:, :], cc_in_k.ap()[:, :])
                        t_d = S_.dma("sync", "sy", dbg["qkv_v"].ap()[:, :], cc_in_v.ap()[:, :])
                        S_.wait("sync", "sy", t_d)
                        S_.barrier()
                    if stop_after != "qkv":
                        emit_ag([(cc_in_q, q_all), (cc_in_k, k_all), (cc_in_v, v_all)])
                        S_.barrier()
                        emit_attention()
                        S_.barrier()
                        if "attn" in dbg:
                            t_d = S_.dma("sync", "sy", dbg["attn"].ap()[:, :], cc_in_a.ap()[:, :])
                            S_.wait("sync", "sy", t_d)
                            S_.barrier()
                        if stop_after != "attn":
                            emit_ag([(cc_in_a, a_all)])
                            S_.barrier()
                            emit_wo()
                            S_.barrier()
                            if stop_after == "wo":
                                _copy_xc_out()
                            else:
                                emit_ln(xc)
                                S_.barrier()
                                emit_routing()
                                S_.barrier()
                                emit_ffn_half(0, xc, out)
                                S_.barrier()
                                emit_ffn_half(1, xc, out)
        S_.barrier()

        block = e(nc.Block())

        @block.sync
        def _(sy):
            for fn in S_.ops["sync"]:
                fn(sy)

        @block.gpsimd
        def _(g):
            for fn in S_.ops["gpsimd"]:
                fn(g)

        @block.tensor
        def _(t):
            for fn in S_.ops["tensor"]:
                fn(t)

        @block.scalar
        def _(sc):
            for fn in S_.ops["scalar"]:
                fn(sc)

        @block.vector
        def _(v):
            for fn in S_.ops["vector"]:
                fn(v)

    return nc


def _prep_shared(w_qkv, w_o, gate_w, gup, dnp, sim):
    gu = np.ascontiguousarray(
        gup.transpose(0, 2, 1).reshape(NE, HKT, 128, 12, 128).transpose(0, 3, 2, 1, 4))
    dn = np.ascontiguousarray(
        dnp.transpose(0, 2, 1).reshape(NE, 6, 128, 16, 128).transpose(0, 3, 2, 1, 4))
    qkv = np.ascontiguousarray(w_qkv.T.reshape(HKT, 128, 48, 128).transpose(2, 1, 0, 3))
    wo = np.ascontiguousarray(w_o.T.reshape(HKT, 128, 16, 128).transpose(2, 1, 0, 3))
    gate = np.ascontiguousarray(gate_w.T.reshape(HKT, 128, NE).transpose(1, 0, 2))
    simc = np.ascontiguousarray(sim)

    cvals = np.zeros((128, 16), np.float32)
    CONSTS = [1.0, 0.5, 0.0, -1.0, 1.0 / H, EPS, -CEXP, SCALE, 2.0, -0.5, 0.25, 0.125, 3.0, 4.0, 5.0, 6.0]
    for i, cv in enumerate(CONSTS):
        cvals[:, i] = cv
    crow = np.zeros((1, 256), np.float32)
    crow[0, 0:128] = 0.5
    crow[0, 128:256] = 1.0
    iota = np.broadcast_to(np.arange(NE, dtype=np.float32), (128, NE)).copy()
    identm = np.eye(128, dtype=np.float32)
    dmaskv = np.zeros((128, 4 * 512), np.float32)
    ki = np.arange(128)[:, None]
    qi = np.arange(512)[None, :]
    for r in range(4):
        dmaskv[:, r * 512:(r + 1) * 512] = np.where(qi >= 128 * r + ki, 0.0, -1e30)
    dmaskv = dmaskv.astype(ml_dtypes.bfloat16)
    return dict(gu_t=gu, dn_t=dn, qkv_t=qkv, wo_t=wo, gate_t=gate, simm=simc,
                cvals=cvals, crow=crow, iota32=iota, ident=identm, dmask=dmaskv)


def _shard_x(x):
    shards = []
    for c in range(NCORES):
        tok = np.concatenate([x[0, c * THALF:(c + 1) * THALF], x[1, c * THALF:(c + 1) * THALF]], axis=0)
        shards.append(np.ascontiguousarray(tok.T))
    return shards


def _unshard_x(outs):
    res = np.empty((B, S, H), np.float32)
    for c in range(NCORES):
        tok = outs[c].T
        res[0, c * THALF:(c + 1) * THALF] = tok[:THALF]
        res[1, c * THALF:(c + 1) * THALF] = tok[THALF:]
    return res


def kernel(L0_attn_hidden_states, L0_attn_w_qkv, L0_attn_w_o, L0_ffn_gate_weight,
           L0_ffn_gate_up_proj, L0_ffn_down_proj, L0_ffn_similarity,
           _stop_after=None, _dev_dump=()):
    shared = _prep_shared(np.asarray(L0_attn_w_qkv, np.float32), np.asarray(L0_attn_w_o, np.float32),
                          np.asarray(L0_ffn_gate_weight, np.float32),
                          np.asarray(L0_ffn_gate_up_proj, np.float32),
                          np.asarray(L0_ffn_down_proj, np.float32),
                          np.asarray(L0_ffn_similarity, np.float32))
    xsh = _shard_x(np.asarray(L0_attn_hidden_states, np.float32))
    nc = build_program(stop_after=_stop_after, dev_dump=_dev_dump)
    in_maps = [dict(shared, x0=xsh[c]) for c in range(NCORES)]
    res = run_bass_kernel_spmd(nc, in_maps, core_ids=list(range(NCORES)))
    if _dev_dump:
        return res
    return _unshard_x([res.results[c]["xout"] for c in range(NCORES)])


# version 03
# speedup vs baseline: 1.0001x; 1.0001x over previous
import numpy as np
import ml_dtypes
import concourse.bass as bass
import concourse.mybir as mybir
from concourse.bass_utils import run_bass_kernel_spmd
from contextlib import ExitStack

F32 = mybir.dt.float32
BF16 = mybir.dt.bfloat16
U32 = mybir.dt.uint32
AF = mybir.ActivationFunctionType
ALU = mybir.AluOpType
AX = mybir.AxisListType

NCORES = 8
H = 2048
HKT = 16
NE = 32
DFF = 768
NH = 16
HD = 128
B = 2
S = 8192
TPC = 2048
THALF = 1024
NTT = 4
CEXP = 20.0
EPS = 1e-5
SCALE = 1.0 / np.sqrt(128.0)

ENGINES = ("sync", "gpsimd", "tensor", "scalar", "vector")


class Sched:

    def __init__(self, sems):
        self.ops = {e: [] for e in ENGINES}
        self.cnt = {name: 0 for name in sems}
        self.sems = sems

    def op(self, eng, fn):
        self.ops[eng].append(fn)

    def inc(self, name, by):
        self.cnt[name] += by
        return self.cnt[name]

    POOL = 6

    def next_dma_sem(self, queue):
        i = self.rr[queue]
        self.rr[queue] = (i + 1) % self.POOL
        name = queue + str(i)
        tgt = self.inc(name, 16)
        return self.sems[name], (name, tgt)

    def dma(self, eng, queue, out_ap, in_ap, **kw):
        semh, tok = self.next_dma_sem(queue)
        self.op(eng, lambda e, o=out_ap, i=in_ap, kw=kw: e.dma_start(out=o, in_=i, **kw).then_inc(semh, 16))
        return tok

    def wait(self, eng, sem_name, val):
        if val is None:
            return
        if isinstance(val, tuple):
            sem_name, val = val
        if val <= 0:
            return
        semh = self.sems[sem_name]
        self.op(eng, lambda e, v=val: e.wait_ge(semh, v))

    def qstate(self, queue):
        return tuple((queue + str(i), self.cnt[queue + str(i)]) for i in range(self.POOL))

    def wait_qstate(self, eng, state):
        for name, v in state:
            if v > 0:
                semh = self.sems[name]
                self.op(eng, lambda e, s=semh, vv=v: e.wait_ge(s, vv))

    def compute(self, eng, sem_name, fn):
        semh = self.sems[sem_name]
        tgt = self.inc(sem_name, 1)
        self.op(eng, lambda e, f=fn: f(e).then_inc(semh, 1))
        return tgt

    def run(self, eng, fn):
        self.op(eng, fn)

    def barrier(self):
        semh = self.sems["bar"]
        tgt = self.inc("bar", len(ENGINES))
        for eng in ENGINES:
            self.op(eng, lambda e: e.nop().then_inc(semh, 1))
            self.op(eng, lambda e, v=tgt: e.wait_ge(semh, v))
        return tgt


def build_program(stop_after=None, dev_dump=()):
    nc = bass.Bass("TRN2", target_bir_lowering=False)

    x0 = nc.dram_tensor("x0", [H, TPC], F32, kind="ExternalInput")
    gu_t = nc.dram_tensor("gu_t", [NE, 12, 128, HKT, 128], F32, kind="ExternalInput")
    dn_t = nc.dram_tensor("dn_t", [NE, 16, 128, 6, 128], F32, kind="ExternalInput")
    qkv_t = nc.dram_tensor("qkv_t", [48, 128, HKT, 128], F32, kind="ExternalInput")
    wo_t = nc.dram_tensor("wo_t", [16, 128, HKT, 128], F32, kind="ExternalInput")
    gate_t = nc.dram_tensor("gate_t", [128, HKT, NE], F32, kind="ExternalInput")
    simm = nc.dram_tensor("simm", [NE, NE], F32, kind="ExternalInput")
    cvals = nc.dram_tensor("cvals", [128, 16], F32, kind="ExternalInput")
    crow = nc.dram_tensor("crow", [1, 256], F32, kind="ExternalInput")
    iota32 = nc.dram_tensor("iota32", [128, NE], F32, kind="ExternalInput")
    ident = nc.dram_tensor("ident", [128, 128], F32, kind="ExternalInput")
    dmask = nc.dram_tensor("dmask", [128, 4 * 512], BF16, kind="ExternalInput")

    out = nc.dram_tensor("xout", [H, TPC], F32, kind="ExternalOutput")

    xc = nc.dram_tensor("xc", [H, TPC], F32)
    cc_in_q = nc.dram_tensor("cc_in_q", [NH * B * 128, THALF], BF16)
    cc_in_k = nc.dram_tensor("cc_in_k", [NH * B * 128, THALF], BF16)
    cc_in_v = nc.dram_tensor("cc_in_v", [NH * B * 128, 8 * 128], BF16)
    cc_in_a = nc.dram_tensor("cc_in_a", [4 * 128, S], BF16)
    q_all = nc.dram_tensor("q_all", [NCORES * NH * B * 128, THALF], BF16, addr_space="Shared")
    k_all = nc.dram_tensor("k_all", [NCORES * NH * B * 128, THALF], BF16, addr_space="Shared")
    v_all = nc.dram_tensor("v_all", [NCORES * NH * B * 128, 8 * 128], BF16, addr_space="Shared")
    a_all = nc.dram_tensor("a_all", [NCORES * 4 * 128, S], BF16, addr_space="Shared")

    dbg = {}
    if "ln" in dev_dump:
        dbg["ln"] = nc.dram_tensor("dbg_ln", [H, TPC], BF16, kind="ExternalOutput")
    if "combine" in dev_dump:
        dbg["combine"] = nc.dram_tensor("dbg_combine", [NE, TPC], F32, kind="ExternalOutput")
    if "qkv" in dev_dump:
        dbg["qkv"] = nc.dram_tensor("dbg_qkv", [NH * B * 128, THALF], BF16, kind="ExternalOutput")
        dbg["qkv_k"] = nc.dram_tensor("dbg_qkv_k", [NH * B * 128, THALF], BF16, kind="ExternalOutput")
        dbg["qkv_v"] = nc.dram_tensor("dbg_qkv_v", [NH * B * 128, 8 * 128], BF16, kind="ExternalOutput")
    if "attn" in dev_dump:
        dbg["attn"] = nc.dram_tensor("dbg_attn", [4 * 128, S], BF16, kind="ExternalOutput")
    if "rt" in dev_dump:
        dbg["rt_probs"] = nc.dram_tensor("dbg_rt_probs", [128, NE], F32, kind="ExternalOutput")
        dbg["rt_vals"] = nc.dram_tensor("dbg_rt_vals", [128, 8], F32, kind="ExternalOutput")
        dbg["rt_idx"] = nc.dram_tensor("dbg_rt_idx", [128, 8], U32, kind="ExternalOutput")
        dbg["rt_rw"] = nc.dram_tensor("dbg_rt_rw", [128, 8], F32, kind="ExternalOutput")
        dbg["rt_acc0"] = nc.dram_tensor("dbg_rt_acc0", [128, NE], F32, kind="ExternalOutput")
        dbg["rt_secT"] = nc.dram_tensor("dbg_rt_secT", [NE, 128], F32, kind="ExternalOutput")
        dbg["rt_cand"] = nc.dram_tensor("dbg_rt_cand", [128, NE], F32, kind="ExternalOutput")

    with ExitStack() as ctx:
        e = ctx.enter_context

        sems = {}
        for name in ("pe", "ac", "ve", "cc", "bar"):
            sems[name] = e(nc.semaphore("s_" + name))
        for q in ("sy", "gp"):
            for i in range(Sched.POOL):
                sems[q + str(i)] = e(nc.semaphore("s_" + q + str(i)))
        S_ = Sched(sems)
        S_.rr = {"sy": 0, "gp": 0}


        A_ln = e(nc.sbuf_tensor("A_ln", [128, HKT * TPC], BF16))
        A_x = e(nc.sbuf_tensor("A_x", [128, HKT * THALF], F32))
        A_h = e(nc.sbuf_tensor("A_h", [128, 2 * 6 * THALF], BF16))
        A_wg = e(nc.sbuf_tensor("A_wg", [128, 4 * HKT * 128], BF16))
        A_wd = e(nc.sbuf_tensor("A_wd", [128, 4 * 6 * 128], BF16))
        cwb = e(nc.sbuf_tensor("cwb", [128, 2 * THALF], F32))
        combineT = e(nc.sbuf_tensor("combineT", [NE, TPC], F32))
        gate_bf = e(nc.sbuf_tensor("gate_bf", [128, HKT * NE], BF16))
        sim_sb = e(nc.sbuf_tensor("sim_sb", [NE, NE], F32))
        cpool = e(nc.sbuf_tensor("cpool", [128, 16], F32))
        cprow = e(nc.sbuf_tensor("cprow", [1, 256], F32))
        iota_sb = e(nc.sbuf_tensor("iota_sb", [128, NE], F32))
        id_f = e(nc.sbuf_tensor("id_f", [128, 128], F32))
        id_b = e(nc.sbuf_tensor("id_b", [128, 128], BF16))
        pid_sb = e(nc.sbuf_tensor("pid_sb", [1, 1], U32))

        sq_sb = e(nc.sbuf_tensor("sq_sb", [128, 2 * 512], F32))
        ln_tmp = e(nc.sbuf_tensor("ln_tmp", [128, 512], F32))

        rt_probs = e(nc.sbuf_tensor("rt_probs", [128, NE], F32))
        rt_vals = e(nc.sbuf_tensor("rt_vals", [128, 8], F32))
        rt_idx = e(nc.sbuf_tensor("rt_idx", [128, 8], U32))
        rt_idxf = e(nc.sbuf_tensor("rt_idxf", [128, 8], F32))
        rt_rw = e(nc.sbuf_tensor("rt_rw", [128, 8], F32))
        rt_oh = e(nc.sbuf_tensor("rt_oh", [128, 8 * NE], F32))
        rt_selb = e(nc.sbuf_tensor("rt_selb", [128, 8 * NE], F32))
        rt_prim = e(nc.sbuf_tensor("rt_prim", [128, NE], F32))
        rt_secT = e(nc.sbuf_tensor("rt_secT", [NE, 128], F32))
        rt_cand = e(nc.sbuf_tensor("rt_cand", [128, NE], F32))
        rt_cm8 = e(nc.sbuf_tensor("rt_cm8", [128, 8], F32))
        rt_ci8 = e(nc.sbuf_tensor("rt_ci8", [128, 8], U32))
        rt_cif = e(nc.sbuf_tensor("rt_cif", [128, 1], F32))
        rt_ohb = e(nc.sbuf_tensor("rt_ohb", [128, NE], F32))
        rt_stay = e(nc.sbuf_tensor("rt_stay", [128, 1], U32))
        rt_sel = e(nc.sbuf_tensor("rt_sel", [128, NE], F32))
        rt_acc = e(nc.sbuf_tensor("rt_acc", [128, NE], F32))
        rt_t1 = e(nc.sbuf_tensor("rt_t1", [128, 1], F32))
        rt_rsum = e(nc.sbuf_tensor("rt_rsum", [128, 1], F32))

        sg_sb = e(nc.sbuf_tensor("sg_sb", [128, 2 * 512], BF16))
        us_sb = e(nc.sbuf_tensor("us_sb", [128, 2 * 512], BF16))

        def _addr(h):
            return nc.lookup_mls(h).memorylocations[0].addr
        ax_a, ah_a, awg_a = _addr(A_x), _addr(A_h), _addr(A_wg)
        awd_a, acw_a = _addr(A_wd), _addr(cwb)
        asg_a, aus_a, asq_a = _addr(sg_sb), _addr(us_sb), _addr(sq_sb)
        stat_sb = nc.alloc_sbuf_tensor_at("stat_sb", [1, 4 * 512], F32, offset=awg_a)
        qbuf = [nc.alloc_sbuf_tensor_at("qbuf0", [128, S], BF16, offset=ax_a),
                nc.alloc_sbuf_tensor_at("qbuf1", [128, S], BF16, offset=ax_a + 16384)]
        kbuf = [nc.alloc_sbuf_tensor_at("kbuf0", [128, S], BF16, offset=ax_a + 32768),
                nc.alloc_sbuf_tensor_at("kbuf1", [128, S], BF16, offset=ax_a + 49152)]
        vbuf = [nc.alloc_sbuf_tensor_at("vbuf0", [128, S], BF16, offset=ah_a),
                nc.alloc_sbuf_tensor_at("vbuf1", [128, S], BF16, offset=awg_a)]
        strip = nc.alloc_sbuf_tensor_at("strip", [1, THALF], F32, offset=asq_a)
        dmask_sb = nc.alloc_sbuf_tensor_at("dmask_sb", [128, 4 * 512], BF16, offset=asq_a)
        denr_sb = nc.alloc_sbuf_tensor_at("denr_sb", [1, 512], F32, offset=_addr(ln_tmp))
        qk_st = nc.alloc_sbuf_tensor_at("qk_st", [128, 2 * 512], BF16, offset=asg_a)
        vt_st = nc.alloc_sbuf_tensor_at("vt_st", [128, 2 * 512], BF16, offset=aus_a)
        e_sb = nc.alloc_sbuf_tensor_at("e_sb", [128, 2 * 512], BF16, offset=acw_a)
        den_sb = nc.alloc_sbuf_tensor_at("den_sb", [128, 2 * 512], F32, offset=acw_a + 2048)
        at_st = nc.alloc_sbuf_tensor_at("at_st", [128, 2 * 512], BF16, offset=acw_a + 6144)
        o_sb = nc.alloc_sbuf_tensor_at("o_sb", [128, 2 * 512], F32, offset=awd_a)
        A_wo = nc.alloc_sbuf_tensor_at("A_wo", [128, 2 * HKT * 128], BF16, offset=acw_a)
        xt_sb = nc.alloc_sbuf_tensor_at("xt_sb", [128, 2 * 512], F32, offset=awd_a)

        pb = [e(nc.psum_tensor("pb%d" % i, [128, 512], F32)) for i in range(8)]

        CONSTS = [1.0, 0.5, 0.0, -1.0, 1.0 / H, EPS, -CEXP, SCALE, 2.0, -0.5, 0.25, 0.125, 3.0, 4.0, 5.0, 6.0]
        for ci, cv in enumerate(CONSTS):
            nc.const_aps.register(F32, cv, cpool[:, ci:ci + 1])

        S_.dma("sync", "sy", pid_sb[:, :], nc.partition_id_tensor.ap())
        S_.dma("sync", "sy", cpool[:, :], cvals[:, :])
        S_.dma("sync", "sy", cprow[:, :], crow[:, :])
        S_.dma("sync", "sy", iota_sb[:, :], iota32[:, :])
        S_.dma("sync", "sy", id_f[:, :], ident[:, :])
        S_.dma("sync", "sy", sim_sb[:, :], simm[:, :])
        S_.dma("gpsimd", "gp", gate_bf[:, :].rearrange("p (k n) -> p k n", k=HKT),
               gate_t.ap())
        S_.dma("gpsimd", "gp", id_b[:, :], ident[:, :])
        S_.wait_qstate("sync", S_.qstate("sy"))

        val = {}

        def _setup_regs(sy):
            val["pid"] = nc.values_load(pid_sb[:1, :1], engines=[mybir.EngineType.SP],
                                        min_val=0, max_val=NCORES - 1,
                                        skip_runtime_bounds_check=True)
        S_.run("sync", _setup_regs)
        S_.barrier()

        def emit_ln(src):
            ones_col = cpool[:, 0:1]
            ones_row = cprow[0:1, 128:256]
            ps_sx = pb[0][0:1, :]
            ps_sxx = pb[1][0:1, :]
            ps_mb = pb[2]
            ps_rb = pb[3]
            t_sq = {}
            t_sxx = {}
            t_norm_last = {}
            t_stats_done = {}
            for tt in range(NTT):
                xst = A_x[:, (tt % 2) * 8192:(tt % 2) * 8192 + 8192]
                if tt >= 2:
                    S_.wait("sync", "ve", t_norm_last[tt - 2])
                t_x = {}
                for kt in range(HKT):
                    t_x[kt] = S_.dma("sync", "sy", xst[:, kt * 512:(kt + 1) * 512],
                                     src.ap()[kt * 128:(kt + 1) * 128, tt * 512:(tt + 1) * 512])
                if tt >= 1:
                    S_.wait("tensor", "ve", t_stats_done[tt - 1])
                for kt in range(HKT):
                    xt = xst[:, kt * 512:(kt + 1) * 512]
                    sq = sq_sb[:, (kt % 2) * 512:(kt % 2) * 512 + 512]
                    S_.wait("scalar", "sy", t_x[kt])
                    if (tt, kt - 2) in t_sxx:
                        S_.wait("scalar", "pe", t_sxx[(tt, kt - 2)])
                    elif tt > 0 and kt < 2:
                        S_.wait("scalar", "pe", t_sxx[(tt - 1, 14 + kt)])
                    t_sq[(tt, kt)] = S_.compute("scalar", "ac", lambda sc, o=sq, i=xt: sc.activation(o, i, AF.Square))
                    S_.wait("tensor", "sy", t_x[kt])
                    S_.run("tensor", lambda t, o=ps_sx, r=xt, st=(kt == 0), sp=(kt == HKT - 1):
                           t.matmul(o, ones_col, r, start=st, stop=sp))
                    S_.wait("tensor", "ac", t_sq[(tt, kt)])
                    t_sxx[(tt, kt)] = S_.compute("tensor", "pe", lambda t, o=ps_sxx, r=sq, st=(kt == 0), sp=(kt == HKT - 1):
                                                 t.matmul(o, ones_col, r, start=st, stop=sp))
                m_r = stat_sb[0:1, 0:512]
                var_r = stat_sb[0:1, 512:1024]
                rstd_r = stat_sb[0:1, 1024:1536]
                tmp_r = stat_sb[0:1, 1536:2048]
                S_.wait("vector", "pe", t_sxx[(tt, HKT - 1)])
                S_.run("vector", lambda v: v.tensor_scalar_mul(m_r, ps_sx, 1.0 / H))
                S_.run("vector", lambda v: v.tensor_scalar_mul(tmp_r, ps_sxx, 1.0 / H))
                S_.run("vector", lambda v: v.drain())
                S_.run("vector", lambda v: v.tensor_mul(var_r, m_r, m_r))
                S_.run("vector", lambda v: v.drain())
                S_.run("vector", lambda v: v.tensor_sub(var_r, tmp_r, var_r))
                S_.run("vector", lambda v: v.drain())
                t_stats_done[tt] = S_.compute("vector", "ve", lambda v: v.tensor_scalar_add(var_r, var_r, EPS))
                S_.wait("scalar", "ve", t_stats_done[tt])
                t_sqr = S_.compute("scalar", "ac", lambda sc: sc.activation(tmp_r, var_r, AF.Sqrt))
                S_.wait("vector", "ac", t_sqr)
                S_.run("vector", lambda v: v.reciprocal(rstd_r, tmp_r))
                t_rs = S_.compute("vector", "ve", lambda v: v.drain())
                if tt >= 1:
                    S_.wait("tensor", "ve", t_norm_last[tt - 1])
                S_.wait("tensor", "ve", t_rs)
                S_.run("tensor", lambda t: t.matmul(ps_mb[:, :], ones_row, m_r, start=True, stop=True))
                t_bc = S_.compute("tensor", "pe", lambda t: t.matmul(ps_rb[:, :], ones_row, rstd_r, start=True, stop=True))
                S_.wait("vector", "pe", t_bc)
                for kt in range(HKT):
                    xt = xst[:, kt * 512:(kt + 1) * 512]
                    lnt = A_ln[:, kt * TPC + tt * 512: kt * TPC + tt * 512 + 512]
                    S_.run("vector", lambda v, x_=xt: v.tensor_sub(ln_tmp[:, :], x_, ps_mb[:, :]))
                    t_norm_last[tt] = S_.compute("vector", "ve", lambda v, o=lnt: v.tensor_mul(o, ln_tmp[:, :], ps_rb[:, :]))
            return t_norm_last[NTT - 1]

        def emit_routing():
            ps_lg = pb[4][:, 0:NE]
            ps_tr = pb[5]
            ps_sr = pb[6][:, 0:NE]
            ps_cmb = pb[7]
            t_ec = None
            for tile in range(HKT):
                t0 = tile * 128
                if t_ec is not None:
                    S_.wait("tensor", "ac", t_ec)
                    S_.wait("tensor", "ve", S_.cnt["ve"])
                for kt in range(HKT):
                    lnt = A_ln[:, kt * TPC + t0: kt * TPC + t0 + 128]
                    gt = gate_bf[:, kt * NE:(kt + 1) * NE]
                    if kt < HKT - 1:
                        S_.run("tensor", lambda t, l=lnt, r=gt, st=(kt == 0): t.matmul(ps_lg, l, r, start=st, stop=False))
                    else:
                        t_lg = S_.compute("tensor", "pe", lambda t, l=lnt, r=gt: t.matmul(ps_lg, l, r, start=False, stop=True))
                S_.wait("vector", "pe", t_lg)
                S_.run("vector", lambda v: v.tensor_reduce(rt_t1[:, :], ps_lg, axis=AX.X, op=ALU.max, negate=True))
                t_evl = S_.compute("vector", "ve", lambda v: v.tensor_copy(rt_probs[:, :], ps_lg))
                S_.wait("scalar", "ve", t_evl)
                t_exp = S_.compute("scalar", "ac", lambda sc: sc.activation(rt_probs[:, :], rt_probs[:, :], AF.Exp,
                                                                            bias=rt_t1[:, :], scale=1.0))
                S_.wait("vector", "ac", t_exp)
                S_.run("vector", lambda v: v.max(out=rt_vals[:, :], in_=rt_probs[:, :]))
                S_.run("vector", lambda v: v.drain())
                S_.run("vector", lambda v: v.max_index(rt_idx[:, :], rt_vals[:, :], rt_probs[:, :]))
                S_.run("vector", lambda v: v.drain())
                S_.run("vector", lambda v: v.tensor_copy(rt_idxf[:, :], rt_idx[:, :]))
                S_.run("vector", lambda v: v.drain())
                S_.run("vector", lambda v: v.tensor_reduce(rt_rsum[:, :], rt_vals[:, :], axis=AX.X, op=ALU.add))
                S_.run("vector", lambda v: v.drain())
                S_.run("vector", lambda v: v.reciprocal(rt_t1[:, :], rt_rsum[:, :]))
                S_.run("vector", lambda v: v.drain())
                S_.run("vector", lambda v: v.tensor_scalar(rt_rw[:, :], rt_vals[:, :], rt_t1[:, :], None, ALU.mult))
                S_.run("vector", lambda v: v.drain())
                for k in range(8):
                    S_.run("vector", lambda v, k=k: v.tensor_scalar(
                        rt_oh[:, k * NE:(k + 1) * NE], iota_sb[:, :], rt_idxf[:, k:k + 1], None, ALU.is_equal))
                for k in range(4):
                    S_.run("vector", lambda v, k=k: v.tensor_scalar(
                        rt_selb[:, k * NE:(k + 1) * NE], rt_oh[:, k * NE:(k + 1) * NE],
                        rt_rw[:, k:k + 1], None, ALU.mult))
                S_.run("vector", lambda v: v.drain())
                S_.run("vector", lambda v: v.tensor_reduce(
                    rt_prim[:, :], rt_oh[:, 0:4 * NE].rearrange("p (k e) -> p e k", k=4), axis=AX.X, op=ALU.add))
                t_oh_ready = S_.compute("vector", "ve", lambda v: v.tensor_copy(rt_cif[:, :], rt_idxf[:, 0:1]))
                if tile == 0 and "rt_probs" in dbg:
                    S_.wait("sync", "ve", t_oh_ready)
                    S_.dma("sync", "sy", dbg["rt_probs"].ap()[:, :], rt_probs[:, :])
                    S_.dma("sync", "sy", dbg["rt_vals"].ap()[:, :], rt_vals[:, :])
                    S_.dma("sync", "sy", dbg["rt_idx"].ap()[:, :], rt_idx[:, :])
                    S_.dma("sync", "sy", dbg["rt_rw"].ap()[:, :], rt_rw[:, :])
                    S_.dma("sync", "sy", dbg["rt_secT"].ap()[0:1, 0:1], rt_rsum[:1, 0:1])
                    t_dd = S_.dma("sync", "sy", dbg["rt_acc0"].ap()[:, :], rt_acc[:, :])
                    S_.wait("vector", "sy", t_dd)
                for j in range(4):
                    ohj = rt_oh[:, (4 + j) * NE:(5 + j) * NE]
                    S_.wait("tensor", "ve", t_oh_ready if j == 0 else t_addacc)
                    t_tr = S_.compute("tensor", "pe", lambda t, i=ohj: t.transpose(ps_tr[0:NE, 0:128], i, id_f[:, :]))
                    S_.wait("scalar", "pe", t_tr)
                    t_ev = S_.compute("scalar", "ac", lambda sc: sc.copy(rt_secT[:, :], ps_tr[0:NE, 0:128]))
                    S_.wait("tensor", "ac", t_ev)
                    t_sr = S_.compute("tensor", "pe", lambda t: t.matmul(ps_sr, rt_secT[:, :], sim_sb[:, :],
                                                                         start=True, stop=True))
                    S_.wait("vector", "pe", t_sr)
                    S_.run("vector", lambda v: v.tensor_mul(rt_cand[:, :], ps_sr, rt_prim[:, :]))
                    if tile == 0 and j == 0 and "rt_cand" in dbg:
                        t_dd2 = S_.compute("vector", "ve", lambda v: v.engine_nop())
                        S_.wait("sync", "ve", t_dd2)
                        S_.dma("sync", "sy", dbg["rt_secT"].ap()[:, :], rt_secT[:, :])
                        t_dd3 = S_.dma("sync", "sy", dbg["rt_cand"].ap()[:, :], rt_cand[:, :])
                        S_.wait("vector", "sy", t_dd3)
                    S_.run("vector", lambda v: v.drain())
                    S_.run("vector", lambda v: v.max(out=rt_cm8[:, :], in_=rt_cand[:, :]))
                    S_.run("vector", lambda v: v.drain())
                    S_.run("vector", lambda v: v.max_index(rt_ci8[:, :], rt_cm8[:, :], rt_cand[:, :]))
                    S_.run("vector", lambda v: v.drain())
                    S_.run("vector", lambda v: v.tensor_copy(rt_cif[:, :], rt_ci8[:, 0:1]))
                    S_.run("vector", lambda v: v.drain())
                    S_.run("vector", lambda v: v.tensor_scalar(rt_ohb[:, :], iota_sb[:, :], rt_cif[:, :], None, ALU.is_equal))
                    S_.run("vector", lambda v: v.drain())
                    S_.run("vector", lambda v: v.tensor_scalar(rt_stay[:, :], rt_cm8[:, 0:1], 0.5, None, ALU.is_lt))
                    S_.run("vector", lambda v: v.drain())
                    S_.run("vector", lambda v, oh=ohj, k=4 + j: v.select(
                        rt_selb[:, k * NE:(k + 1) * NE], rt_stay[:, 0:1].to_broadcast([128, NE]), oh, rt_ohb[:, :]))
                    S_.run("vector", lambda v: v.drain())
                    t_addacc = S_.compute("vector", "ve", lambda v, k=4 + j: v.tensor_scalar(
                        rt_selb[:, k * NE:(k + 1) * NE], rt_selb[:, k * NE:(k + 1) * NE],
                        rt_rw[:, k:k + 1], None, ALU.mult))
                S_.run("vector", lambda v: v.drain())
                t_addacc = S_.compute("vector", "ve", lambda v: v.tensor_reduce(
                    rt_acc[:, :], rt_selb[:, :].rearrange("p (k e) -> p e k", k=8), axis=AX.X, op=ALU.add))
                S_.wait("tensor", "ve", t_addacc)
                t_tc = S_.compute("tensor", "pe", lambda t: t.transpose(ps_cmb[0:NE, 0:128], rt_acc[:, :], id_f[:, :]))
                S_.wait("scalar", "pe", t_tc)
                t_ec = S_.compute("scalar", "ac", lambda sc, c0=t0: sc.copy(combineT[:, c0:c0 + 128], ps_cmb[0:NE, 0:128]))
            S_.wait("vector", "ac", t_ec)
            return t_ec

        def emit_ffn_half(T, src, dst):
            x_half = A_x
            t_xl = 0
            for g4 in range(4):
                t_xl = S_.dma("sync", "sy",
                              x_half[:, g4 * 4 * THALF:(g4 + 1) * 4 * THALF].rearrange("p (k t) -> p k t", k=4),
                              src.ap().rearrange("(kt p) t -> p kt t", p=128)[:, g4 * 4:(g4 + 1) * 4,
                                                                             T * THALF:(T + 1) * THALF])

            pg = [pb[0], pb[1]]
            pu = [pb[2], pb[3]]
            pdmap = {(0, 0): pb[4], (0, 1): pb[5], (1, 0): pb[6], (1, 1): pb[7]}
            pcw = [pb[4], pb[5]]

            half_row = cprow[0:1, 0:128]

            gu_jobs = [(ee, j) for ee in range(NE) for j in range(6)]
            gu_done = {}
            t_gmm = {}
            t_umm = {}
            t_silu = {}
            t_us = {}
            t_h = {}
            t_dmm = {}
            t_xacc = {}
            t_cwev = {}

            def gu_buf(j, gi):
                slot = (j % 2) * 2 + gi
                return A_wg[:, slot * 2048: slot * 2048 + 2048]

            def wd_buf(ee, o):
                slot = (ee % 2) * 2 + (o % 2)
                return A_wd[:, slot * 768: slot * 768 + 768]

            def emit_gu_dma(idx):
                if idx >= len(gu_jobs):
                    return
                ee, j = gu_jobs[idx]
                if idx >= 2:
                    pee, pj = gu_jobs[idx - 2]
                    S_.wait("gpsimd", "pe", t_umm[(pee, pj, 1)])
                for gi, wslot in ((0, j), (1, 6 + j)):
                    gu_done[(ee, j)] = S_.dma("gpsimd", "gp",
                                              gu_buf(j, gi).rearrange("p (k n) -> p k n", k=HKT),
                                              gu_t.ap()[ee, wslot])

            wd_jobs = [(p_, o_, ei_) for p_ in range(NE // 2) for o_ in range(16) for ei_ in range(2)]
            wd_done = {}

            def emit_wd_dma(idx):
                if idx >= len(wd_jobs):
                    return
                p_, o_, ei_ = wd_jobs[idx]
                ee = 2 * p_ + ei_
                if idx >= 4:
                    pp, po, pei = wd_jobs[idx - 4]
                    S_.wait("gpsimd", "pe", t_dmm[(pp, po)])
                wd_done[(ee, o_)] = S_.dma("gpsimd", "gp",
                                           wd_buf(ee, o_).rearrange("p (k n) -> p k n", k=6),
                                           dn_t.ap()[ee, o_])

            emit_gu_dma(0)
            emit_gu_dma(1)
            gu_pf = 2
            wd_pf = 0
            t_cwmm_prev = 0
            S_.wait("vector", "sy", t_xl)

            for p in range(NE // 2):
                pair = (2 * p, 2 * p + 1)
                for ei, ee in enumerate(pair):
                    st_buf = strip[0:1, 0:THALF]
                    if ee > 0:
                        S_.wait("sync", "pe", t_cwmm_prev)
                    t_strip = S_.dma("sync", "sy", st_buf, combineT[ee:ee + 1, T * THALF:(T + 1) * THALF])
                    S_.wait("tensor", "sy", t_strip)
                    if p > 0:
                        S_.wait("tensor", "ve", t_xacc[(p - 1, 15, 1)])
                        S_.wait("scalar", "ve", t_us[(2 * (p - 1) + ei, 5, 1)])
                    for th in range(2):
                        if th == 0:
                            S_.run("tensor", lambda t, r=st_buf[0:1, 0:512]: t.matmul(pcw[0][:, :], half_row, r, start=True, stop=True))
                        else:
                            t_cwmm = S_.compute("tensor", "pe",
                                                lambda t, r=st_buf[0:1, 512:1024]: t.matmul(pcw[1][:, :], half_row, r, start=True, stop=True))
                    t_cwmm_prev = t_cwmm
                    S_.wait("scalar", "pe", t_cwmm)
                    for th in range(2):
                        t_cwev[ee] = S_.compute("scalar", "ac",
                                                lambda sc, o=cwb[:, ei * THALF + th * 512: ei * THALF + th * 512 + 512],
                                                i=pcw[th][:, :]: sc.copy(o, i))

                for ei, ee in enumerate(pair):
                    hb = A_h[:, ei * 6 * THALF:(ei + 1) * 6 * THALF]
                    for j in range(6):
                        if p == 0 and ei == 0 and j == 0:
                            for _ in range(4):
                                emit_wd_dma(wd_pf); wd_pf += 1
                        gbuf, ubuf = gu_buf(j, 0), gu_buf(j, 1)
                        S_.wait("tensor", "gp", gu_done[(ee, j)])
                        for gi, wb_, pgu, tdict in ((0, gbuf, pg, t_gmm), (1, ubuf, pu, t_umm)):
                            for th in range(2):
                                prev = (ee, j - 1, th) if j >= 1 else ((ee - 1, 5, th) if ee > 0 else None)
                                if prev is not None and prev in t_silu:
                                    if gi == 0:
                                        S_.wait("tensor", "ac", t_silu[prev])
                                    else:
                                        S_.wait("tensor", "ve", t_us[prev])
                                rhs0 = kt_rhs = None
                                for kt in range(HKT):
                                    rhs = A_ln[:, kt * TPC + T * THALF + th * 512: kt * TPC + T * THALF + th * 512 + 512]
                                    if kt < HKT - 1:
                                        S_.run("tensor", lambda t, o=pgu[th][:, :], l=wb_[:, kt * 128:(kt + 1) * 128], r=rhs,
                                               st=(kt == 0): t.matmul(o, l, r, start=st, stop=False))
                                    else:
                                        tdict[(ee, j, th)] = S_.compute(
                                            "tensor", "pe",
                                            lambda t, o=pgu[th][:, :], l=wb_[:, kt * 128:(kt + 1) * 128], r=rhs:
                                            t.matmul(o, l, r, start=False, stop=True))
                        if j < 5:
                            emit_gu_dma(gu_pf); gu_pf += 1
                        for th in range(2):
                            S_.wait("scalar", "pe", t_gmm[(ee, j, th)])
                            hprev = (ee, j - 1, th) if j >= 1 else ((ee - 1, 5, th) if ee > 0 else None)
                            if hprev is not None and hprev in t_h:
                                S_.wait("scalar", "ve", t_h[hprev])
                            t_silu[(ee, j, th)] = S_.compute("scalar", "ac",
                                                             lambda sc, o=sg_sb[:, th * 512:(th + 1) * 512], i=pg[th][:, :]:
                                                             sc.activation(o, i, AF.Silu))
                            S_.wait("vector", "pe", t_umm[(ee, j, th)])
                            S_.wait("vector", "ac", t_cwev[ee])
                            t_us[(ee, j, th)] = S_.compute("vector", "ve",
                                                           lambda v, o=us_sb[:, th * 512:(th + 1) * 512], i=pu[th][:, :],
                                                           c=cwb[:, ei * THALF + th * 512: ei * THALF + th * 512 + 512]:
                                                           v.scalar_tensor_tensor(o, i, 1.0, c, ALU.mult, ALU.mult))
                            S_.wait("vector", "ac", t_silu[(ee, j, th)])
                            t_h[(ee, j, th)] = S_.compute("vector", "ve",
                                                          lambda v, o=hb[:, j * THALF + th * 512: j * THALF + th * 512 + 512],
                                                          a=us_sb[:, th * 512:(th + 1) * 512],
                                                          b=sg_sb[:, th * 512:(th + 1) * 512]:
                                                          v.tensor_mul(o, a, b))
                    emit_gu_dma(gu_pf); gu_pf += 1
                S_.wait("tensor", "ve", t_h[(pair[1], 5, 1)])
                S_.wait("tensor", "ac", t_cwev[pair[1]])
                for o in range(16):
                    for ei, ee in enumerate(pair):
                        S_.wait("tensor", "gp", wd_done[(ee, o)])
                        if ei == 0:
                            if o >= 2:
                                S_.wait("tensor", "ve", t_xacc[(p, o - 2, 1)])
                            elif p > 0:
                                S_.wait("tensor", "ve", t_xacc[(p - 1, 14 + o, 1)])
                        wbuf = wd_buf(ee, o)
                        hb = A_h[:, ei * 6 * THALF:(ei + 1) * 6 * THALF]
                        for kt in range(6):
                            for th in range(2):
                                lm = (ei == 1 and kt == 5 and th == 1)
                                if not lm:
                                    S_.run("tensor", lambda t, o_=pdmap[(o % 2, th)][:, :],
                                           l=wbuf[:, kt * 128:(kt + 1) * 128],
                                           r=hb[:, kt * THALF + th * 512: kt * THALF + th * 512 + 512],
                                           st=(ei == 0 and kt == 0): t.matmul(o_, l, r, start=st, stop=False))
                                else:
                                    t_dmm[(p, o)] = S_.compute("tensor", "pe",
                                                               lambda t, o_=pdmap[(o % 2, th)][:, :],
                                                               l=wbuf[:, kt * 128:(kt + 1) * 128],
                                                               r=hb[:, kt * THALF + th * 512: kt * THALF + th * 512 + 512]:
                                                               t.matmul(o_, l, r, start=False, stop=True))
                    S_.wait("vector", "pe", t_dmm[(p, o)])
                    for th in range(2):
                        t_xacc[(p, o, th)] = S_.compute(
                            "vector", "ve",
                            lambda v, xo=x_half[:, o * THALF + th * 512: o * THALF + th * 512 + 512],
                            ps=pdmap[(o % 2, th)][:, :]: v.tensor_add(xo, xo, ps))
                    emit_wd_dma(wd_pf); wd_pf += 1
                    emit_wd_dma(wd_pf); wd_pf += 1
            S_.wait("sync", "ve", t_xacc[(NE // 2 - 1, 15, 1)])
            t_xs = 0
            for g4 in range(4):
                t_xs = S_.dma("sync", "sy",
                              dst.ap().rearrange("(kt p) t -> p kt t", p=128)[:, g4 * 4:(g4 + 1) * 4,
                                                                             T * THALF:(T + 1) * THALF],
                              x_half[:, g4 * 4 * THALF:(g4 + 1) * 4 * THALF].rearrange("p (k t) -> p k t", k=4))
            S_.wait("sync", "sy", t_xs)
            return

        def emit_qkv():
            pq = [pb[0], pb[1]]
            ptr_b = pb[2].ap().bitcast(BF16)
            t_wdma = {}
            t_mm = {}
            t_ev = {}
            t_vtr = {}
            t_vev = {}
            t_st = {}

            def wbuf(o):
                return A_wg[:, (o % 4) * 2048:(o % 4) * 2048 + 2048]

            def emit_wdma(o):
                if o >= 48:
                    return
                if o >= 4:
                    S_.wait("gpsimd", "pe", t_mm[(o - 4, NTT - 1)])
                t_wdma[o] = S_.dma("gpsimd", "gp", wbuf(o).rearrange("p (k n) -> p k n", k=HKT),
                                   qkv_t.ap()[o])

            for oo in range(4):
                emit_wdma(oo)
            for o in range(48):
                typ, head = o // 16, o % 16
                S_.wait("tensor", "gp", t_wdma[o])
                for tt in range(NTT):
                    bb, tof = tt // 2, (tt % 2) * 512
                    if (o, tt - 2) in t_ev:
                        S_.wait("tensor", "ac", t_ev[(o, tt - 2)])
                    elif o > 0 and tt < 2:
                        S_.wait("tensor", "ac", t_ev[(o - 1, 2 + tt)])
                    for kt in range(HKT):
                        rhs = A_ln[:, kt * TPC + tt * 512: kt * TPC + tt * 512 + 512]
                        if kt < HKT - 1:
                            S_.run("tensor", lambda t, l=wbuf(o)[:, kt * 128:(kt + 1) * 128], r=rhs,
                                   o_=pq[tt % 2][:, :], st=(kt == 0): t.matmul(o_, l, r, start=st, stop=False))
                        else:
                            t_mm[(o, tt)] = S_.compute("tensor", "pe",
                                                       lambda t, l=wbuf(o)[:, kt * 128:(kt + 1) * 128], r=rhs,
                                                       o_=pq[tt % 2][:, :]: t.matmul(o_, l, r, start=False, stop=True))
                    S_.wait("scalar", "pe", t_mm[(o, tt)])
                    if (o, tt - 2) in t_st:
                        S_.wait("scalar", "sy", t_st[(o, tt - 2)])
                    elif o > 0 and tt < 2:
                        S_.wait("scalar", "sy", t_st[(o - 1, 2 + tt)])
                    t_ev[(o, tt)] = S_.compute("scalar", "ac",
                                               lambda sc, o_=qk_st[:, (tt % 2) * 512:(tt % 2) * 512 + 512],
                                               i=pq[tt % 2][:, :]: sc.copy(o_, i))
                    if typ < 2:
                        dst = cc_in_q if typ == 0 else cc_in_k
                        S_.wait("sync", "ac", t_ev[(o, tt)])
                        t_st[(o, tt)] = S_.dma("sync", "sy",
                                               dst.ap()[(head * 2 + bb) * 128:(head * 2 + bb) * 128 + 128,
                                                        tof:tof + 512],
                                               qk_st[:, (tt % 2) * 512:(tt % 2) * 512 + 512])
                    else:
                        S_.wait("tensor", "ac", t_ev[(o, tt)])
                        if (o, tt - 2) in t_vev:
                            S_.wait("tensor", "ac", t_vev[(o, tt - 2)])
                        elif o > 32 and tt < 2:
                            S_.wait("tensor", "ac", t_vev[(o - 1, 2 + tt)])
                        for tb in range(4):
                            t_vtr[(o, tt)] = S_.compute("tensor", "pe",
                                                        lambda t, i=qk_st[:, (tt % 2) * 512 + tb * 128:
                                                                          (tt % 2) * 512 + tb * 128 + 128],
                                                        o_=ptr_b[:, tb * 128:(tb + 1) * 128]:
                                                        t.transpose(o_, i, id_b[:, :]))
                        S_.wait("scalar", "pe", t_vtr[(o, tt)])
                        if (o, tt - 2) in t_st:
                            S_.wait("scalar", "sy", t_st[(o, tt - 2)])
                        elif o > 32 and tt < 2:
                            S_.wait("scalar", "sy", t_st[(o - 1, 2 + tt)])
                        t_vev[(o, tt)] = S_.compute("scalar", "ac",
                                                    lambda sc, o_=vt_st[:, (tt % 2) * 512:(tt % 2) * 512 + 512],
                                                    i=ptr_b[:, 0:512]: sc.copy(o_, i))
                        S_.wait("sync", "ac", t_vev[(o, tt)])
                        for tb in range(4):
                            ttg = (tt % 2) * 4 + tb
                            t_st[(o, tt)] = S_.dma("sync", "sy",
                                                   cc_in_v.ap()[(head * 2 + bb) * 128:
                                                                (head * 2 + bb) * 128 + 128,
                                                                ttg * 128:(ttg + 1) * 128],
                                                   vt_st[:, (tt % 2) * 512 + tb * 128:(tt % 2) * 512 + tb * 128 + 128])
                emit_wdma(o + 4)
            S_.wait_qstate("sync", S_.qstate("sy"))
            return

        def emit_ag(pairs):
            n = 0
            for src_t, dst_t in pairs:
                S_.compute("gpsimd", "cc",
                           lambda g, s=src_t, d=dst_t: g.collective_compute(
                               "AllGather", ALU.bypass, replica_groups=[list(range(NCORES))],
                               ins=[s.ap().opt()], outs=[d.ap().opt()]))
                n += 1
            S_.wait("gpsimd", "cc", S_.cnt["cc"])
            return

        def emit_attention():
            t_dm = S_.dma("sync", "sy", dmask_sb[:, :], dmask.ap())
            pS = [pb[0], pb[1]]
            pO = [pb[2], pb[3]]
            pDen = pb[4]
            pB = pb[5]
            t_ld = {}
            t_exp = {}
            t_den = {}
            t_pv = {}
            t_maskv = {}
            t_scl = {}
            t_ost = {}
            t_rec = {}
            t_dred = {}
            last_rec = 0
            last_bc = 0
            last_scl = 0
            last_dred = {}

            def emit_job_loads(j):
                if j >= 4:
                    return
                jb = j % 2
                bb, hh = j % 2, j // 2
                if j >= 2:
                    S_.wait("sync", "pe", t_pv[(j - 2, 15)])
                def _dyndma(sy, semh, off_fn, src_t, patt, dst_ap):
                    sy.dma_start(out=dst_ap, in_=bass.AP(src_t, off_fn(), patt)).then_inc(semh, 16)
                for tens, buf in ((q_all, qbuf[jb]), (k_all, kbuf[jb])):
                    dst = buf[:, :].rearrange("p (s t) -> p s t", s=NCORES)
                    semh_d, _tok = S_.next_dma_sem("sy")
                    S_.run("sync", lambda sy, sh=semh_d, jj=j, s=tens, d=dst:
                           _dyndma(sy, sh, lambda: (val["pid"] * 4 + jj) * (128 * THALF),
                                   s, [[THALF, 128], [4096 * THALF, NCORES], [1, THALF]], d))
                dstv = vbuf[jb][:, :].rearrange("p (s td) -> p s td", s=NCORES)
                semh_d, _tok = S_.next_dma_sem("sy")
                S_.run("sync", lambda sy, sh=semh_d, jj=j, d=dstv:
                       _dyndma(sy, sh, lambda: (val["pid"] * 4 + (jj // 2) * 2 + (jj % 2)) * (128 * 1024),
                               v_all, [[1024, 128], [4096 * 1024, NCORES], [1, 1024]], d))
                t_ld[j] = S_.qstate("sy")

            emit_job_loads(0)
            emit_job_loads(1)
            S_.wait("vector", "sy", t_dm)
            ikb = 0
            for j in range(4):
                jb = j % 2
                S_.wait_qstate("tensor", t_ld[j])
                for qt in range(16):
                    nkb = 4 * qt + 4
                    qsl = qbuf[jb][:, qt * 512:(qt + 1) * 512]
                    if (j, qt - 2) in t_scl:
                        S_.wait("tensor", "ve", t_scl[(j, qt - 2)])
                    elif j > 0 and qt < 2:
                        S_.wait("tensor", "ve", t_scl[(j - 1, 14 + qt)])
                    for kb in range(nkb):
                        par = ikb % 2
                        if ikb >= 2:
                            S_.wait("tensor", "ac", t_exp[ikb - 2])
                        ksl = kbuf[jb][:, kb * 128:(kb + 1) * 128]
                        vsl = vbuf[jb][:, kb * 128:(kb + 1) * 128]
                        t_qk = S_.compute("tensor", "pe",
                                          lambda t, l=ksl, r=qsl, o_=pS[par][:, :]:
                                          t.matmul(o_, l, r, start=True, stop=True))
                        r = kb - 4 * qt
                        if r >= 0:
                            S_.wait("vector", "pe", t_qk)
                            t_maskv[ikb] = S_.compute("vector", "ve",
                                                      lambda v, o_=pS[par][:, :],
                                                      m=dmask_sb[:, r * 512:(r + 1) * 512]:
                                                      v.tensor_add(o_, o_, m))
                            S_.wait("scalar", "ve", t_maskv[ikb])
                        else:
                            S_.wait("scalar", "pe", t_qk)
                        if ikb >= 2:
                            S_.wait("scalar", "pe", t_pv[ikb - 2] if (ikb - 2) in t_pv else 0)
                            S_.wait("scalar", "ve", t_den[ikb - 2])
                        t_exp[ikb] = S_.compute("scalar", "ac",
                                                lambda sc, i=pS[par][:, :],
                                                o_=e_sb[:, par * 512:(par + 1) * 512]:
                                                sc.activation(o_, i, AF.Exp, bias=cpool[:, 6:7], scale=SCALE))
                        S_.wait("vector", "ac", t_exp[ikb])
                        if kb == 0:
                            S_.wait("vector", "pe", last_dred.get(qt % 2, 0))
                            S_.run("vector", lambda v, d=den_sb[:, (qt % 2) * 512:(qt % 2) * 512 + 512]:
                                   v.memset(d, 0.0))
                        t_den[ikb] = S_.compute("vector", "ve",
                                                lambda v, d=den_sb[:, (qt % 2) * 512:(qt % 2) * 512 + 512],
                                                s=e_sb[:, par * 512:(par + 1) * 512]:
                                                v.tensor_add(d, d, s))
                        S_.wait("tensor", "ac", t_exp[ikb])
                        t_pv[(j, qt)] = t_pv[ikb] = S_.compute("tensor", "pe",
                                                               lambda t, l=vsl,
                                                               r=e_sb[:, par * 512:(par + 1) * 512],
                                                               o_=pO[qt % 2][:, :], st=(kb == 0):
                                                               t.matmul(o_, l, r, start=st, stop=(kb == nkb - 1)))
                        ikb += 1
                    S_.wait("tensor", "ve", last_rec)
                    S_.wait("tensor", "ve", t_den[ikb - 1])
                    t_dred[(j, qt)] = last_dred[qt % 2] = S_.compute(
                        "tensor", "pe",
                        lambda t, r=den_sb[:, (qt % 2) * 512:(qt % 2) * 512 + 512]:
                        t.matmul(pDen[0:1, :], cpool[:, 0:1], r, start=True, stop=True))
                    S_.wait("vector", "pe", t_dred[(j, qt)])
                    S_.wait("vector", "pe", last_bc)
                    S_.run("vector", lambda v: v.reciprocal(denr_sb[0:1, :], pDen[0:1, :]))
                    t_rec[(j, qt)] = last_rec = S_.compute("vector", "ve", lambda v: v.drain())
                    S_.wait("tensor", "ve", t_rec[(j, qt)])
                    S_.wait("tensor", "ve", last_scl)
                    t_bc = last_bc = S_.compute("tensor", "pe",
                                      lambda t: t.matmul(pB[:, :], cprow[0:1, 128:256],
                                                         denr_sb[0:1, :], start=True, stop=True))
                    S_.wait("scalar", "pe", t_pv[(j, qt)])
                    if (j, qt - 2) in t_ost:
                        S_.wait("scalar", "sy", t_ost[(j, qt - 2)])
                    elif j > 0 and qt < 2:
                        S_.wait("scalar", "sy", t_ost[(j - 1, 14 + qt)])
                    t_oev = S_.compute("scalar", "ac",
                                       lambda sc, i=pO[qt % 2][:, :],
                                       o_=o_sb[:, (qt % 2) * 512:(qt % 2) * 512 + 512]: sc.copy(o_, i))
                    S_.wait("vector", "ac", t_oev)
                    S_.wait("vector", "pe", t_bc)
                    t_scl[(j, qt)] = last_scl = S_.compute("vector", "ve",
                                                lambda v, a=o_sb[:, (qt % 2) * 512:(qt % 2) * 512 + 512],
                                                o_=at_st[:, (qt % 2) * 512:(qt % 2) * 512 + 512]:
                                                v.tensor_mul(o_, a, pB[:, :]))
                    S_.wait("sync", "ve", t_scl[(j, qt)])
                    t_ost[(j, qt)] = S_.dma("sync", "sy",
                                            cc_in_a.ap()[j * 128:(j + 1) * 128, qt * 512:(qt + 1) * 512],
                                            at_st[:, (qt % 2) * 512:(qt % 2) * 512 + 512])
                emit_job_loads(j + 2)
            S_.wait_qstate("sync", S_.qstate("sy"))
            return

        def emit_wo():
            def _xdma(sy, semh):
                dst = A_ln[:, :].rearrange("p (g t) -> p g t", g=NH * B)
                sy.dma_start(out=dst, in_=bass.AP(a_all, val["pid"] * THALF,
                                                  [[S, 128], [128 * S, NH * B], [1, THALF]])).then_inc(semh, 16)
            semh_x, _tok = S_.next_dma_sem("sy")
            S_.run("sync", lambda sy: _xdma(sy, semh_x))
            t_ex = S_.qstate("sy")
            S_.wait_qstate("tensor", t_ex)

            pw = [pb[0], pb[1]]
            t_wdma = {}
            t_mm = {}
            t_add = {}
            t_xl = {}
            t_xs = {}

            def wobuf(o):
                return A_wo[:, (o % 2) * 2048:(o % 2) * 2048 + 2048]

            def emit_wodma(o):
                if o >= 16:
                    return
                if o >= 2:
                    S_.wait("gpsimd", "pe", t_mm[(o - 2, NTT - 1)])
                t_wdma[o] = S_.dma("gpsimd", "gp", wobuf(o).rearrange("p (k n) -> p k n", k=HKT),
                                   wo_t.ap()[o])

            emit_wodma(0)
            emit_wodma(1)
            for o in range(16):
                S_.wait("tensor", "gp", t_wdma[o])
                for tt in range(NTT):
                    if (o, tt) not in t_xl:
                        pass
                    if (o, tt - 2) in t_xs:
                        S_.wait("sync", "sy", t_xs[(o, tt - 2)])
                    elif o > 0 and tt < 2:
                        S_.wait("sync", "sy", t_xs[(o - 1, 2 + tt)])
                    t_xl[(o, tt)] = S_.dma("sync", "sy", xt_sb[:, (tt % 2) * 512:(tt % 2) * 512 + 512],
                                           xc.ap()[o * 128:(o + 1) * 128, tt * 512:(tt + 1) * 512])
                    if (o, tt - 2) in t_add:
                        S_.wait("tensor", "ve", t_add[(o, tt - 2)])
                    elif o > 0 and tt < 2:
                        S_.wait("tensor", "ve", t_add[(o - 1, 2 + tt)])
                    for kt in range(HKT):
                        rhs = A_ln[:, kt * TPC + tt * 512: kt * TPC + tt * 512 + 512]
                        if kt < HKT - 1:
                            S_.run("tensor", lambda t, l=wobuf(o)[:, kt * 128:(kt + 1) * 128], r=rhs,
                                   o_=pw[tt % 2][:, :], st=(kt == 0): t.matmul(o_, l, r, start=st, stop=False))
                        else:
                            t_mm[(o, tt)] = S_.compute("tensor", "pe",
                                                       lambda t, l=wobuf(o)[:, kt * 128:(kt + 1) * 128], r=rhs,
                                                       o_=pw[tt % 2][:, :]: t.matmul(o_, l, r, start=False, stop=True))
                    S_.wait("vector", "pe", t_mm[(o, tt)])
                    S_.wait("vector", "sy", t_xl[(o, tt)])
                    t_add[(o, tt)] = S_.compute("vector", "ve",
                                                lambda v, x_=xt_sb[:, (tt % 2) * 512:(tt % 2) * 512 + 512],
                                                p_=pw[tt % 2][:, :]: v.tensor_add(x_, x_, p_))
                    S_.wait("sync", "ve", t_add[(o, tt)])
                    t_xs[(o, tt)] = S_.dma("sync", "sy", xc.ap()[o * 128:(o + 1) * 128, tt * 512:(tt + 1) * 512],
                                           xt_sb[:, (tt % 2) * 512:(tt % 2) * 512 + 512])
                emit_wodma(o + 2)
            S_.wait_qstate("sync", S_.qstate("sy"))
            return

        if stop_after == "attn_only":
            emit_attention()
            stop_after = "done"
        if stop_after == "done":
            pass
        elif True:
            emit_ln(x0)
        S_.barrier()
        if stop_after == "done":
            pass
        elif "ln" in dbg:
            t_d = 0
            for kt in range(HKT):
                t_d = S_.dma("sync", "sy", dbg["ln"].ap()[kt * 128:(kt + 1) * 128, :], A_ln[:, kt * TPC:(kt + 1) * TPC])
            S_.wait("sync", "sy", t_d)
            S_.barrier()
        if stop_after == "done":
            pass
        elif stop_after != "ln1":
            emit_routing()
            S_.barrier()
            if "combine" in dbg:
                t_d = S_.dma("sync", "sy", dbg["combine"].ap()[:, :], combineT[:, :])
                S_.wait("sync", "sy", t_d)
                S_.barrier()
            if stop_after != "routing1":
                emit_ffn_half(0, x0, xc)
                S_.barrier()
                emit_ffn_half(1, x0, xc)
                S_.barrier()

                def _copy_xc_out():
                    t_o = 0
                    for g4 in range(4):
                        t_o = S_.dma("sync", "sy",
                                     out.ap().rearrange("(kt p) t -> p kt t", p=128)[:, g4 * 4:(g4 + 1) * 4, :],
                                     xc.ap().rearrange("(kt p) t -> p kt t", p=128)[:, g4 * 4:(g4 + 1) * 4, :])
                    S_.wait("sync", "sy", t_o)

                if stop_after == "ffn1":
                    _copy_xc_out()
                else:
                    emit_ln(xc)
                    S_.barrier()
                    emit_qkv()
                    S_.barrier()
                    if "qkv" in dbg:
                        S_.dma("sync", "sy", dbg["qkv"].ap()[:, :], cc_in_q.ap()[:, :])
                        S_.dma("sync", "sy", dbg["qkv_k"].ap()[:, :], cc_in_k.ap()[:, :])
                        t_d = S_.dma("sync", "sy", dbg["qkv_v"].ap()[:, :], cc_in_v.ap()[:, :])
                        S_.wait("sync", "sy", t_d)
                        S_.barrier()
                    if stop_after != "qkv":
                        emit_ag([(cc_in_q, q_all), (cc_in_k, k_all), (cc_in_v, v_all)])
                        S_.barrier()
                        emit_attention()
                        S_.barrier()
                        if "attn" in dbg:
                            t_d = S_.dma("sync", "sy", dbg["attn"].ap()[:, :], cc_in_a.ap()[:, :])
                            S_.wait("sync", "sy", t_d)
                            S_.barrier()
                        if stop_after != "attn":
                            emit_ag([(cc_in_a, a_all)])
                            S_.barrier()
                            emit_wo()
                            S_.barrier()
                            if stop_after == "wo":
                                _copy_xc_out()
                            else:
                                emit_ln(xc)
                                S_.barrier()
                                emit_routing()
                                S_.barrier()
                                emit_ffn_half(0, xc, out)
                                S_.barrier()
                                emit_ffn_half(1, xc, out)
        S_.barrier()

        block = e(nc.Block())

        @block.sync
        def _(sy):
            for fn in S_.ops["sync"]:
                fn(sy)

        @block.gpsimd
        def _(g):
            for fn in S_.ops["gpsimd"]:
                fn(g)

        @block.tensor
        def _(t):
            for fn in S_.ops["tensor"]:
                fn(t)

        @block.scalar
        def _(sc):
            for fn in S_.ops["scalar"]:
                fn(sc)

        @block.vector
        def _(v):
            for fn in S_.ops["vector"]:
                fn(v)

    return nc


def _prep_shared(w_qkv, w_o, gate_w, gup, dnp, sim):
    gu = np.ascontiguousarray(
        gup.transpose(0, 2, 1).reshape(NE, HKT, 128, 12, 128).transpose(0, 3, 2, 1, 4))
    dn = np.ascontiguousarray(
        dnp.transpose(0, 2, 1).reshape(NE, 6, 128, 16, 128).transpose(0, 3, 2, 1, 4))
    qkv = np.ascontiguousarray(w_qkv.T.reshape(HKT, 128, 48, 128).transpose(2, 1, 0, 3))
    wo = np.ascontiguousarray(w_o.T.reshape(HKT, 128, 16, 128).transpose(2, 1, 0, 3))
    gate = np.ascontiguousarray(gate_w.T.reshape(HKT, 128, NE).transpose(1, 0, 2))
    simc = np.ascontiguousarray(sim)

    cvals = np.zeros((128, 16), np.float32)
    CONSTS = [1.0, 0.5, 0.0, -1.0, 1.0 / H, EPS, -CEXP, SCALE, 2.0, -0.5, 0.25, 0.125, 3.0, 4.0, 5.0, 6.0]
    for i, cv in enumerate(CONSTS):
        cvals[:, i] = cv
    crow = np.zeros((1, 256), np.float32)
    crow[0, 0:128] = 0.5
    crow[0, 128:256] = 1.0
    iota = np.broadcast_to(np.arange(NE, dtype=np.float32), (128, NE)).copy()
    identm = np.eye(128, dtype=np.float32)
    dmaskv = np.zeros((128, 4 * 512), np.float32)
    ki = np.arange(128)[:, None]
    qi = np.arange(512)[None, :]
    for r in range(4):
        dmaskv[:, r * 512:(r + 1) * 512] = np.where(qi >= 128 * r + ki, 0.0, -1e30)
    dmaskv = dmaskv.astype(ml_dtypes.bfloat16)
    return dict(gu_t=gu, dn_t=dn, qkv_t=qkv, wo_t=wo, gate_t=gate, simm=simc,
                cvals=cvals, crow=crow, iota32=iota, ident=identm, dmask=dmaskv)


def _shard_x(x):
    shards = []
    for c in range(NCORES):
        tok = np.concatenate([x[0, c * THALF:(c + 1) * THALF], x[1, c * THALF:(c + 1) * THALF]], axis=0)
        shards.append(np.ascontiguousarray(tok.T))
    return shards


def _unshard_x(outs):
    res = np.empty((B, S, H), np.float32)
    for c in range(NCORES):
        tok = outs[c].T
        res[0, c * THALF:(c + 1) * THALF] = tok[:THALF]
        res[1, c * THALF:(c + 1) * THALF] = tok[THALF:]
    return res


def kernel(L0_attn_hidden_states, L0_attn_w_qkv, L0_attn_w_o, L0_ffn_gate_weight,
           L0_ffn_gate_up_proj, L0_ffn_down_proj, L0_ffn_similarity,
           _stop_after=None, _dev_dump=()):
    shared = _prep_shared(np.asarray(L0_attn_w_qkv, np.float32), np.asarray(L0_attn_w_o, np.float32),
                          np.asarray(L0_ffn_gate_weight, np.float32),
                          np.asarray(L0_ffn_gate_up_proj, np.float32),
                          np.asarray(L0_ffn_down_proj, np.float32),
                          np.asarray(L0_ffn_similarity, np.float32))
    xsh = _shard_x(np.asarray(L0_attn_hidden_states, np.float32))
    nc = build_program(stop_after=_stop_after, dev_dump=_dev_dump)
    in_maps = [dict(shared, x0=xsh[c]) for c in range(NCORES)]
    res = run_bass_kernel_spmd(nc, in_maps, core_ids=list(range(NCORES)))
    if _dev_dump:
        return res
    return _unshard_x([res.results[c]["xout"] for c in range(NCORES)])


# version 04
# speedup vs baseline: 1.1237x; 1.1235x over previous
import numpy as np
import ml_dtypes
import concourse.bass as bass
import concourse.mybir as mybir
from concourse.bass_utils import run_bass_kernel_spmd
from contextlib import ExitStack

F32 = mybir.dt.float32
BF16 = mybir.dt.bfloat16
U32 = mybir.dt.uint32
AF = mybir.ActivationFunctionType
ALU = mybir.AluOpType
AX = mybir.AxisListType

NCORES = 8
H = 2048
HKT = 16
NE = 32
DFF = 768
NH = 16
HD = 128
B = 2
S = 8192
TPC = 2048
THALF = 1024
NTT = 4
CEXP = 20.0
EPS = 1e-5
SCALE = 1.0 / np.sqrt(128.0)

ENGINES = ("sync", "gpsimd", "tensor", "scalar", "vector")


class Sched:

    def __init__(self, sems):
        self.ops = {e: [] for e in ENGINES}
        self.cnt = {name: 0 for name in sems}
        self.sems = sems

    def op(self, eng, fn):
        self.ops[eng].append(fn)

    def inc(self, name, by):
        self.cnt[name] += by
        return self.cnt[name]

    POOL = 6

    def next_dma_sem(self, queue):
        i = self.rr[queue]
        self.rr[queue] = (i + 1) % self.POOL
        name = queue + str(i)
        tgt = self.inc(name, 16)
        return self.sems[name], (name, tgt)

    def dma(self, eng, queue, out_ap, in_ap, **kw):
        semh, tok = self.next_dma_sem(queue)
        self.op(eng, lambda e, o=out_ap, i=in_ap, kw=kw: e.dma_start(out=o, in_=i, **kw).then_inc(semh, 16))
        return tok

    def wait(self, eng, sem_name, val):
        if val is None:
            return
        if isinstance(val, tuple):
            sem_name, val = val
        if val <= 0:
            return
        semh = self.sems[sem_name]
        self.op(eng, lambda e, v=val: e.wait_ge(semh, v))

    def qstate(self, queue):
        return tuple((queue + str(i), self.cnt[queue + str(i)]) for i in range(self.POOL))

    def wait_qstate(self, eng, state):
        for name, v in state:
            if v > 0:
                semh = self.sems[name]
                self.op(eng, lambda e, s=semh, vv=v: e.wait_ge(s, vv))

    def compute(self, eng, sem_name, fn):
        semh = self.sems[sem_name]
        tgt = self.inc(sem_name, 1)
        self.op(eng, lambda e, f=fn: f(e).then_inc(semh, 1))
        return tgt

    def run(self, eng, fn):
        self.op(eng, fn)

    def barrier(self):
        semh = self.sems["bar"]
        tgt = self.inc("bar", len(ENGINES))
        for eng in ENGINES:
            self.op(eng, lambda e: e.nop().then_inc(semh, 1))
            self.op(eng, lambda e, v=tgt: e.wait_ge(semh, v))
        return tgt


def build_program(stop_after=None, dev_dump=()):
    nc = bass.Bass("TRN2", target_bir_lowering=False)

    x0 = nc.dram_tensor("x0", [H, TPC], F32, kind="ExternalInput")
    gu_t = nc.dram_tensor("gu_t", [NE, 12, 128, HKT, 128], F32, kind="ExternalInput")
    dn_t = nc.dram_tensor("dn_t", [NE, 16, 128, 6, 128], F32, kind="ExternalInput")
    qkv_t = nc.dram_tensor("qkv_t", [48, 128, HKT, 128], F32, kind="ExternalInput")
    wo_t = nc.dram_tensor("wo_t", [16, 128, HKT, 128], F32, kind="ExternalInput")
    gate_t = nc.dram_tensor("gate_t", [128, HKT, NE], F32, kind="ExternalInput")
    simm = nc.dram_tensor("simm", [NE, NE], F32, kind="ExternalInput")
    cvals = nc.dram_tensor("cvals", [128, 16], F32, kind="ExternalInput")
    crow = nc.dram_tensor("crow", [1, 256], F32, kind="ExternalInput")
    iota32 = nc.dram_tensor("iota32", [128, NE], F32, kind="ExternalInput")
    pcol = nc.dram_tensor("pcol", [NE, 1], F32, kind="ExternalInput")
    ident = nc.dram_tensor("ident", [128, 128], F32, kind="ExternalInput")
    dmask = nc.dram_tensor("dmask", [128, 4 * 512], BF16, kind="ExternalInput")

    out = nc.dram_tensor("xout", [H, TPC], F32, kind="ExternalOutput")

    xc = nc.dram_tensor("xc", [H, TPC], F32)
    cc_in_q = nc.dram_tensor("cc_in_q", [NH * B * 128, THALF], BF16)
    cc_in_k = nc.dram_tensor("cc_in_k", [NH * B * 128, THALF], BF16)
    cc_in_v = nc.dram_tensor("cc_in_v", [NH * B * 128, 8 * 128], BF16)
    cc_in_a = nc.dram_tensor("cc_in_a", [4 * 128, S], BF16)
    q_all = nc.dram_tensor("q_all", [NCORES * NH * B * 128, THALF], BF16, addr_space="Shared")
    k_all = nc.dram_tensor("k_all", [NCORES * NH * B * 128, THALF], BF16, addr_space="Shared")
    v_all = nc.dram_tensor("v_all", [NCORES * NH * B * 128, 8 * 128], BF16, addr_space="Shared")
    a_all = nc.dram_tensor("a_all", [NCORES * 4 * 128, S], BF16, addr_space="Shared")

    dbg = {}
    if "ln" in dev_dump:
        dbg["ln"] = nc.dram_tensor("dbg_ln", [H, TPC], BF16, kind="ExternalOutput")
    if "combine" in dev_dump:
        dbg["combine"] = nc.dram_tensor("dbg_combine", [NE, TPC], F32, kind="ExternalOutput")
    if "qkv" in dev_dump:
        dbg["qkv"] = nc.dram_tensor("dbg_qkv", [NH * B * 128, THALF], BF16, kind="ExternalOutput")
        dbg["qkv_k"] = nc.dram_tensor("dbg_qkv_k", [NH * B * 128, THALF], BF16, kind="ExternalOutput")
        dbg["qkv_v"] = nc.dram_tensor("dbg_qkv_v", [NH * B * 128, 8 * 128], BF16, kind="ExternalOutput")
    if "attn" in dev_dump:
        dbg["attn"] = nc.dram_tensor("dbg_attn", [4 * 128, S], BF16, kind="ExternalOutput")
    if "rt" in dev_dump:
        dbg["rt_probs"] = nc.dram_tensor("dbg_rt_probs", [128, NE], F32, kind="ExternalOutput")
        dbg["rt_vals"] = nc.dram_tensor("dbg_rt_vals", [128, 8], F32, kind="ExternalOutput")
        dbg["rt_idx"] = nc.dram_tensor("dbg_rt_idx", [128, 8], U32, kind="ExternalOutput")
        dbg["rt_rw"] = nc.dram_tensor("dbg_rt_rw", [128, 8], F32, kind="ExternalOutput")
        dbg["rt_acc0"] = nc.dram_tensor("dbg_rt_acc0", [128, NE], F32, kind="ExternalOutput")
        dbg["rt_secT"] = nc.dram_tensor("dbg_rt_secT", [NE, 128], F32, kind="ExternalOutput")
        dbg["rt_cand"] = nc.dram_tensor("dbg_rt_cand", [128, NE], F32, kind="ExternalOutput")

    with ExitStack() as ctx:
        e = ctx.enter_context

        sems = {}
        for name in ("pe", "ac", "ve", "cc", "bar"):
            sems[name] = e(nc.semaphore("s_" + name))
        for q in ("sy", "gp"):
            for i in range(Sched.POOL):
                sems[q + str(i)] = e(nc.semaphore("s_" + q + str(i)))
        S_ = Sched(sems)
        S_.rr = {"sy": 0, "gp": 0}


        A_ln = e(nc.sbuf_tensor("A_ln", [128, HKT * TPC], BF16))
        A_x = e(nc.sbuf_tensor("A_x", [128, HKT * THALF], F32))
        A_h = e(nc.sbuf_tensor("A_h", [128, 2 * 6 * THALF], BF16))
        A_wg = e(nc.sbuf_tensor("A_wg", [128, 4 * HKT * 128], BF16))
        A_wd = e(nc.sbuf_tensor("A_wd", [128, 4 * 6 * 128], BF16))
        cwb = e(nc.sbuf_tensor("cwb", [128, 2 * THALF], F32))
        combineT = e(nc.sbuf_tensor("combineT", [NE, TPC], F32))
        gate_bf = e(nc.sbuf_tensor("gate_bf", [128, HKT * NE], BF16))
        sim_sb = e(nc.sbuf_tensor("sim_sb", [NE, NE], F32))
        cpool = e(nc.sbuf_tensor("cpool", [128, 16], F32))
        cprow = e(nc.sbuf_tensor("cprow", [1, 256], F32))
        iota_sb = e(nc.sbuf_tensor("iota_sb", [128, NE], F32))
        id_f = e(nc.sbuf_tensor("id_f", [128, 128], F32))
        id_b = e(nc.sbuf_tensor("id_b", [128, 128], BF16))
        pid_sb = e(nc.sbuf_tensor("pid_sb", [1, 1], U32))
        pcol_sb = e(nc.sbuf_tensor("pcol_sb", [NE, 1], F32))

        sq_sb = e(nc.sbuf_tensor("sq_sb", [128, 2 * 512], F32))
        ln_tmp = e(nc.sbuf_tensor("ln_tmp", [128, 512], F32))

        rt_probs = e(nc.sbuf_tensor("rt_probs", [128, NE], F32))
        rt_vals = e(nc.sbuf_tensor("rt_vals", [128, 8], F32))
        rt_idx = e(nc.sbuf_tensor("rt_idx", [128, 8], U32))
        rt_idxf = e(nc.sbuf_tensor("rt_idxf", [128, 8], F32))
        rt_rw = e(nc.sbuf_tensor("rt_rw", [128, 8], F32))
        rt_oh = e(nc.sbuf_tensor("rt_oh", [128, 8 * NE], F32))
        rt_selb = e(nc.sbuf_tensor("rt_selb", [128, 8 * NE], F32))
        rt_prim = e(nc.sbuf_tensor("rt_prim", [128, NE], F32))
        rt_secT = e(nc.sbuf_tensor("rt_secT", [NE, 128], F32))
        rt_cand = e(nc.sbuf_tensor("rt_cand", [128, NE], F32))
        rt_cm8 = e(nc.sbuf_tensor("rt_cm8", [128, 8], F32))
        rt_ci8 = e(nc.sbuf_tensor("rt_ci8", [128, 8], U32))
        rt_cif = e(nc.sbuf_tensor("rt_cif", [128, 1], F32))
        rt_ohb = e(nc.sbuf_tensor("rt_ohb", [128, NE], F32))
        rt_stay = e(nc.sbuf_tensor("rt_stay", [128, 1], U32))
        rt_sel = e(nc.sbuf_tensor("rt_sel", [128, NE], F32))
        rt_acc = e(nc.sbuf_tensor("rt_acc", [128, NE], F32))
        rt_t1 = e(nc.sbuf_tensor("rt_t1", [128, 1], F32))
        rt_rsum = e(nc.sbuf_tensor("rt_rsum", [128, 1], F32))

        sg_sb = e(nc.sbuf_tensor("sg_sb", [128, 2 * 512], BF16))
        us_sb = e(nc.sbuf_tensor("us_sb", [128, 2 * 512], BF16))

        def _addr(h):
            return nc.lookup_mls(h).memorylocations[0].addr
        ax_a, ah_a, awg_a = _addr(A_x), _addr(A_h), _addr(A_wg)
        awd_a, acw_a = _addr(A_wd), _addr(cwb)
        asg_a, aus_a, asq_a = _addr(sg_sb), _addr(us_sb), _addr(sq_sb)
        stat_sb = nc.alloc_sbuf_tensor_at("stat_sb", [1, 4 * 512], F32, offset=awg_a)
        qbuf = [nc.alloc_sbuf_tensor_at("qbuf0", [128, S], BF16, offset=ax_a),
                nc.alloc_sbuf_tensor_at("qbuf1", [128, S], BF16, offset=ax_a + 16384)]
        kbuf = [nc.alloc_sbuf_tensor_at("kbuf0", [128, S], BF16, offset=ax_a + 32768),
                nc.alloc_sbuf_tensor_at("kbuf1", [128, S], BF16, offset=ax_a + 49152)]
        vbuf = [nc.alloc_sbuf_tensor_at("vbuf0", [128, S], BF16, offset=ah_a),
                nc.alloc_sbuf_tensor_at("vbuf1", [128, S], BF16, offset=awg_a)]
        sel_sb = nc.alloc_sbuf_tensor_at("sel_sb", [NE, 2 * 128], F32, offset=asq_a)
        dmask_sb = nc.alloc_sbuf_tensor_at("dmask_sb", [128, 4 * 512], BF16, offset=asq_a)
        denr_sb = nc.alloc_sbuf_tensor_at("denr_sb", [1, 512], F32, offset=_addr(ln_tmp))
        qk_st = nc.alloc_sbuf_tensor_at("qk_st", [128, 2 * 512], BF16, offset=asg_a)
        vt_st = nc.alloc_sbuf_tensor_at("vt_st", [128, 2 * 512], BF16, offset=aus_a)
        e_sb = nc.alloc_sbuf_tensor_at("e_sb", [128, 4 * 512], BF16, offset=acw_a)
        den_sb = nc.alloc_sbuf_tensor_at("den_sb", [128, 2 * 512], F32, offset=acw_a + 4096)
        o_sb = nc.alloc_sbuf_tensor_at("o_sb", [128, 2 * 512], F32, offset=awd_a)
        at_st = nc.alloc_sbuf_tensor_at("at_st", [128, 2 * 512], BF16, offset=awd_a + 4096)
        A_wo = nc.alloc_sbuf_tensor_at("A_wo", [128, 4 * HKT * 128], BF16, offset=ax_a)
        xt_sb = nc.alloc_sbuf_tensor_at("xt_sb", [128, 2 * 512], F32, offset=awd_a)

        pb = [e(nc.psum_tensor("pb%d" % i, [128, 512], F32)) for i in range(8)]

        CONSTS = [1.0, 0.5, 0.0, -1.0, 1.0 / H, EPS, -CEXP, SCALE, 2.0, -0.5, 0.25, 0.125, 3.0, 4.0, 5.0, 6.0]
        for ci, cv in enumerate(CONSTS):
            nc.const_aps.register(F32, cv, cpool[:, ci:ci + 1])

        S_.dma("sync", "sy", pid_sb[:, :], nc.partition_id_tensor.ap())
        S_.dma("sync", "sy", cpool[:, :], cvals[:, :])
        S_.dma("sync", "sy", cprow[:, :], crow[:, :])
        S_.dma("sync", "sy", iota_sb[:, :], iota32[:, :])
        S_.dma("sync", "sy", pcol_sb[:, :], pcol.ap())
        S_.dma("sync", "sy", id_f[:, :], ident[:, :])
        S_.dma("sync", "sy", sim_sb[:, :], simm[:, :])
        S_.dma("gpsimd", "gp", gate_bf[:, :].rearrange("p (k n) -> p k n", k=HKT),
               gate_t.ap())
        S_.dma("gpsimd", "gp", id_b[:, :], ident[:, :])
        S_.wait_qstate("sync", S_.qstate("sy"))

        val = {}

        def _setup_regs(sy):
            val["pid"] = nc.values_load(pid_sb[:1, :1], engines=[mybir.EngineType.SP],
                                        min_val=0, max_val=NCORES - 1,
                                        skip_runtime_bounds_check=True)
        S_.run("sync", _setup_regs)
        S_.barrier()

        def emit_ln(src):
            ones_col = cpool[:, 0:1]
            ones_row = cprow[0:1, 128:256]
            ps_sx = pb[0][0:1, :]
            ps_sxx = pb[1][0:1, :]
            ps_mb = pb[2]
            ps_rb = pb[3]
            t_sq = {}
            t_sxx = {}
            t_norm_last = {}
            t_stats_done = {}
            for tt in range(NTT):
                xst = A_x[:, (tt % 2) * 8192:(tt % 2) * 8192 + 8192]
                if tt >= 2:
                    S_.wait("sync", "ve", t_norm_last[tt - 2])
                t_x = {}
                for kt in range(HKT):
                    t_x[kt] = S_.dma("sync", "sy", xst[:, kt * 512:(kt + 1) * 512],
                                     src.ap()[kt * 128:(kt + 1) * 128, tt * 512:(tt + 1) * 512])
                if tt >= 1:
                    S_.wait("tensor", "ve", t_stats_done[tt - 1])
                for kt in range(HKT):
                    xt = xst[:, kt * 512:(kt + 1) * 512]
                    sq = sq_sb[:, (kt % 2) * 512:(kt % 2) * 512 + 512]
                    S_.wait("scalar", "sy", t_x[kt])
                    if (tt, kt - 2) in t_sxx:
                        S_.wait("scalar", "pe", t_sxx[(tt, kt - 2)])
                    elif tt > 0 and kt < 2:
                        S_.wait("scalar", "pe", t_sxx[(tt - 1, 14 + kt)])
                    t_sq[(tt, kt)] = S_.compute("scalar", "ac", lambda sc, o=sq, i=xt: sc.activation(o, i, AF.Square))
                    S_.wait("tensor", "sy", t_x[kt])
                    S_.run("tensor", lambda t, o=ps_sx, r=xt, st=(kt == 0), sp=(kt == HKT - 1):
                           t.matmul(o, ones_col, r, start=st, stop=sp))
                    S_.wait("tensor", "ac", t_sq[(tt, kt)])
                    t_sxx[(tt, kt)] = S_.compute("tensor", "pe", lambda t, o=ps_sxx, r=sq, st=(kt == 0), sp=(kt == HKT - 1):
                                                 t.matmul(o, ones_col, r, start=st, stop=sp))
                m_r = stat_sb[0:1, 0:512]
                var_r = stat_sb[0:1, 512:1024]
                rstd_r = stat_sb[0:1, 1024:1536]
                tmp_r = stat_sb[0:1, 1536:2048]
                S_.wait("vector", "pe", t_sxx[(tt, HKT - 1)])
                S_.run("vector", lambda v: v.tensor_scalar_mul(m_r, ps_sx, 1.0 / H))
                S_.run("vector", lambda v: v.tensor_scalar_mul(tmp_r, ps_sxx, 1.0 / H))
                S_.run("vector", lambda v: v.drain())
                S_.run("vector", lambda v: v.tensor_mul(var_r, m_r, m_r))
                S_.run("vector", lambda v: v.drain())
                S_.run("vector", lambda v: v.tensor_sub(var_r, tmp_r, var_r))
                S_.run("vector", lambda v: v.drain())
                t_stats_done[tt] = S_.compute("vector", "ve", lambda v: v.tensor_scalar_add(var_r, var_r, EPS))
                S_.wait("scalar", "ve", t_stats_done[tt])
                t_sqr = S_.compute("scalar", "ac", lambda sc: sc.activation(tmp_r, var_r, AF.Sqrt))
                S_.wait("vector", "ac", t_sqr)
                S_.run("vector", lambda v: v.reciprocal(rstd_r, tmp_r))
                t_rs = S_.compute("vector", "ve", lambda v: v.drain())
                if tt >= 1:
                    S_.wait("tensor", "ve", t_norm_last[tt - 1])
                S_.wait("tensor", "ve", t_rs)
                S_.run("tensor", lambda t: t.matmul(ps_mb[:, :], ones_row, m_r, start=True, stop=True))
                t_bc = S_.compute("tensor", "pe", lambda t: t.matmul(ps_rb[:, :], ones_row, rstd_r, start=True, stop=True))
                S_.wait("vector", "pe", t_bc)
                for kt in range(HKT):
                    xt = xst[:, kt * 512:(kt + 1) * 512]
                    lnt = A_ln[:, kt * TPC + tt * 512: kt * TPC + tt * 512 + 512]
                    S_.run("vector", lambda v, x_=xt: v.tensor_sub(ln_tmp[:, :], x_, ps_mb[:, :]))
                    t_norm_last[tt] = S_.compute("vector", "ve", lambda v, o=lnt: v.tensor_mul(o, ln_tmp[:, :], ps_rb[:, :]))
            return t_norm_last[NTT - 1]

        def emit_routing():
            ps_lg = pb[4][:, 0:NE]
            ps_tr = pb[5]
            ps_sr = pb[6][:, 0:NE]
            ps_cmb = pb[7]
            t_ec = None
            for tile in range(HKT):
                t0 = tile * 128
                if t_ec is not None:
                    S_.wait("tensor", "ac", t_ec)
                    S_.wait("tensor", "ve", S_.cnt["ve"])
                for kt in range(HKT):
                    lnt = A_ln[:, kt * TPC + t0: kt * TPC + t0 + 128]
                    gt = gate_bf[:, kt * NE:(kt + 1) * NE]
                    if kt < HKT - 1:
                        S_.run("tensor", lambda t, l=lnt, r=gt, st=(kt == 0): t.matmul(ps_lg, l, r, start=st, stop=False))
                    else:
                        t_lg = S_.compute("tensor", "pe", lambda t, l=lnt, r=gt: t.matmul(ps_lg, l, r, start=False, stop=True))
                S_.wait("vector", "pe", t_lg)
                S_.run("vector", lambda v: v.tensor_reduce(rt_t1[:, :], ps_lg, axis=AX.X, op=ALU.max, negate=True))
                t_evl = S_.compute("vector", "ve", lambda v: v.tensor_copy(rt_probs[:, :], ps_lg))
                S_.wait("scalar", "ve", t_evl)
                t_exp = S_.compute("scalar", "ac", lambda sc: sc.activation(rt_probs[:, :], rt_probs[:, :], AF.Exp,
                                                                            bias=rt_t1[:, :], scale=1.0))
                S_.wait("vector", "ac", t_exp)
                S_.run("vector", lambda v: v.max(out=rt_vals[:, :], in_=rt_probs[:, :]))
                S_.run("vector", lambda v: v.drain())
                S_.run("vector", lambda v: v.max_index(rt_idx[:, :], rt_vals[:, :], rt_probs[:, :]))
                S_.run("vector", lambda v: v.drain())
                S_.run("vector", lambda v: v.tensor_copy(rt_idxf[:, :], rt_idx[:, :]))
                S_.run("vector", lambda v: v.drain())
                S_.run("vector", lambda v: v.tensor_reduce(rt_rsum[:, :], rt_vals[:, :], axis=AX.X, op=ALU.add))
                S_.run("vector", lambda v: v.drain())
                S_.run("vector", lambda v: v.reciprocal(rt_t1[:, :], rt_rsum[:, :]))
                S_.run("vector", lambda v: v.drain())
                S_.run("vector", lambda v: v.tensor_scalar(rt_rw[:, :], rt_vals[:, :], rt_t1[:, :], None, ALU.mult))
                S_.run("vector", lambda v: v.drain())
                for k in range(8):
                    S_.run("vector", lambda v, k=k: v.tensor_scalar(
                        rt_oh[:, k * NE:(k + 1) * NE], iota_sb[:, :], rt_idxf[:, k:k + 1], None, ALU.is_equal))
                for k in range(4):
                    S_.run("vector", lambda v, k=k: v.tensor_scalar(
                        rt_selb[:, k * NE:(k + 1) * NE], rt_oh[:, k * NE:(k + 1) * NE],
                        rt_rw[:, k:k + 1], None, ALU.mult))
                S_.run("vector", lambda v: v.drain())
                S_.run("vector", lambda v: v.tensor_reduce(
                    rt_prim[:, :], rt_oh[:, 0:4 * NE].rearrange("p (k e) -> p e k", k=4), axis=AX.X, op=ALU.add))
                t_oh_ready = S_.compute("vector", "ve", lambda v: v.tensor_copy(rt_cif[:, :], rt_idxf[:, 0:1]))
                if tile == 0 and "rt_probs" in dbg:
                    S_.wait("sync", "ve", t_oh_ready)
                    S_.dma("sync", "sy", dbg["rt_probs"].ap()[:, :], rt_probs[:, :])
                    S_.dma("sync", "sy", dbg["rt_vals"].ap()[:, :], rt_vals[:, :])
                    S_.dma("sync", "sy", dbg["rt_idx"].ap()[:, :], rt_idx[:, :])
                    S_.dma("sync", "sy", dbg["rt_rw"].ap()[:, :], rt_rw[:, :])
                    S_.dma("sync", "sy", dbg["rt_secT"].ap()[0:1, 0:1], rt_rsum[:1, 0:1])
                    t_dd = S_.dma("sync", "sy", dbg["rt_acc0"].ap()[:, :], rt_acc[:, :])
                    S_.wait("vector", "sy", t_dd)
                for j in range(4):
                    ohj = rt_oh[:, (4 + j) * NE:(5 + j) * NE]
                    S_.wait("tensor", "ve", t_oh_ready if j == 0 else t_addacc)
                    t_tr = S_.compute("tensor", "pe", lambda t, i=ohj: t.transpose(ps_tr[0:NE, 0:128], i, id_f[:, :]))
                    S_.wait("scalar", "pe", t_tr)
                    t_ev = S_.compute("scalar", "ac", lambda sc: sc.copy(rt_secT[:, :], ps_tr[0:NE, 0:128]))
                    S_.wait("tensor", "ac", t_ev)
                    t_sr = S_.compute("tensor", "pe", lambda t: t.matmul(ps_sr, rt_secT[:, :], sim_sb[:, :],
                                                                         start=True, stop=True))
                    S_.wait("vector", "pe", t_sr)
                    S_.run("vector", lambda v: v.tensor_mul(rt_cand[:, :], ps_sr, rt_prim[:, :]))
                    if tile == 0 and j == 0 and "rt_cand" in dbg:
                        t_dd2 = S_.compute("vector", "ve", lambda v: v.engine_nop())
                        S_.wait("sync", "ve", t_dd2)
                        S_.dma("sync", "sy", dbg["rt_secT"].ap()[:, :], rt_secT[:, :])
                        t_dd3 = S_.dma("sync", "sy", dbg["rt_cand"].ap()[:, :], rt_cand[:, :])
                        S_.wait("vector", "sy", t_dd3)
                    S_.run("vector", lambda v: v.drain())
                    S_.run("vector", lambda v: v.max(out=rt_cm8[:, :], in_=rt_cand[:, :]))
                    S_.run("vector", lambda v: v.drain())
                    S_.run("vector", lambda v: v.max_index(rt_ci8[:, :], rt_cm8[:, :], rt_cand[:, :]))
                    S_.run("vector", lambda v: v.drain())
                    S_.run("vector", lambda v: v.tensor_copy(rt_cif[:, :], rt_ci8[:, 0:1]))
                    S_.run("vector", lambda v: v.drain())
                    S_.run("vector", lambda v: v.tensor_scalar(rt_ohb[:, :], iota_sb[:, :], rt_cif[:, :], None, ALU.is_equal))
                    S_.run("vector", lambda v: v.drain())
                    S_.run("vector", lambda v: v.tensor_scalar(rt_stay[:, :], rt_cm8[:, 0:1], 0.5, None, ALU.is_lt))
                    S_.run("vector", lambda v: v.drain())
                    S_.run("vector", lambda v, oh=ohj, k=4 + j: v.select(
                        rt_selb[:, k * NE:(k + 1) * NE], rt_stay[:, 0:1].to_broadcast([128, NE]), oh, rt_ohb[:, :]))
                    S_.run("vector", lambda v: v.drain())
                    t_addacc = S_.compute("vector", "ve", lambda v, k=4 + j: v.tensor_scalar(
                        rt_selb[:, k * NE:(k + 1) * NE], rt_selb[:, k * NE:(k + 1) * NE],
                        rt_rw[:, k:k + 1], None, ALU.mult))
                S_.run("vector", lambda v: v.drain())
                t_addacc = S_.compute("vector", "ve", lambda v: v.tensor_reduce(
                    rt_acc[:, :], rt_selb[:, :].rearrange("p (k e) -> p e k", k=8), axis=AX.X, op=ALU.add))
                S_.wait("tensor", "ve", t_addacc)
                t_tc = S_.compute("tensor", "pe", lambda t: t.transpose(ps_cmb[0:NE, 0:128], rt_acc[:, :], id_f[:, :]))
                S_.wait("scalar", "pe", t_tc)
                t_ec = S_.compute("scalar", "ac", lambda sc, c0=t0: sc.copy(combineT[:, c0:c0 + 128], ps_cmb[0:NE, 0:128]))
            S_.wait("vector", "ac", t_ec)
            return t_ec

        def emit_ffn_half(T, src, dst):
            x_half = A_x
            t_xl = 0
            for g4 in range(4):
                t_xl = S_.dma("sync", "sy",
                              x_half[:, g4 * 4 * THALF:(g4 + 1) * 4 * THALF].rearrange("p (k t) -> p k t", k=4),
                              src.ap().rearrange("(kt p) t -> p kt t", p=128)[:, g4 * 4:(g4 + 1) * 4,
                                                                             T * THALF:(T + 1) * THALF])

            pg = [pb[0], pb[1]]
            pu = [pb[2], pb[3]]
            pdmap = {(0, 0): pb[4], (0, 1): pb[5], (1, 0): pb[6], (1, 1): pb[7]}
            pcw = [pb[4], pb[5]]

            half_row = cprow[0:1, 0:128]

            gu_jobs = [(ee, j) for ee in range(NE) for j in range(6)]
            gu_done = {}
            t_gmm = {}
            t_umm = {}
            t_silu = {}
            t_us = {}
            t_h = {}
            t_dmm = {}
            t_xacc = {}
            t_cwev = {}

            def gu_buf(j, gi):
                slot = (j % 2) * 2 + gi
                return A_wg[:, slot * 2048: slot * 2048 + 2048]

            def wd_buf(ee, o):
                slot = (ee % 2) * 2 + (o % 2)
                return A_wd[:, slot * 768: slot * 768 + 768]

            def emit_gu_dma(idx):
                if idx >= len(gu_jobs):
                    return
                ee, j = gu_jobs[idx]
                if idx >= 2:
                    pee, pj = gu_jobs[idx - 2]
                    S_.wait("gpsimd", "pe", t_umm[(pee, pj, 1)])
                for gi, wslot in ((0, j), (1, 6 + j)):
                    gu_done[(ee, j)] = S_.dma("gpsimd", "gp",
                                              gu_buf(j, gi).rearrange("p (k n) -> p k n", k=HKT),
                                              gu_t.ap()[ee, wslot])

            wd_jobs = [(p_, o_, ei_) for p_ in range(NE // 2) for o_ in range(16) for ei_ in range(2)]
            wd_done = {}

            def emit_wd_dma(idx):
                if idx >= len(wd_jobs):
                    return
                p_, o_, ei_ = wd_jobs[idx]
                ee = 2 * p_ + ei_
                if idx >= 4:
                    pp, po, pei = wd_jobs[idx - 4]
                    S_.wait("gpsimd", "pe", t_dmm[(pp, po)])
                wd_done[(ee, o_)] = S_.dma("gpsimd", "gp",
                                           wd_buf(ee, o_).rearrange("p (k n) -> p k n", k=6),
                                           dn_t.ap()[ee, o_])

            emit_gu_dma(0)
            emit_gu_dma(1)
            gu_pf = 2
            wd_pf = 0
            t_cwmm_prev = 0
            S_.wait("vector", "sy", t_xl)

            for p in range(NE // 2):
                pair = (2 * p, 2 * p + 1)
                for ei, ee in enumerate(pair):
                    if ee > 0:
                        S_.wait("vector", "pe", t_cwmm_prev)
                    t_sel = S_.compute("vector", "ve",
                                       lambda v, o=sel_sb[0:NE, ei * 128:(ei + 1) * 128], c=float(ee):
                                       v.tensor_scalar(o, pcol_sb[0:NE, 0:1].to_broadcast([NE, 128]),
                                                       c, 0.5, ALU.is_equal, ALU.mult))
                    S_.wait("tensor", "ve", t_sel)
                    if p > 0:
                        S_.wait("tensor", "ve", t_xacc[(p - 1, 15, 1)])
                        S_.wait("scalar", "ve", t_us[(2 * (p - 1) + ei, 5, 1)])
                    for th in range(2):
                        lsel = sel_sb[0:NE, ei * 128:(ei + 1) * 128]
                        rcmb = combineT[0:NE, T * THALF + th * 512: T * THALF + th * 512 + 512]
                        if th == 0:
                            S_.run("tensor", lambda t, l=lsel, r=rcmb: t.matmul(pcw[0][:, :], l, r, start=True, stop=True))
                        else:
                            t_cwmm = S_.compute("tensor", "pe",
                                                lambda t, l=lsel, r=rcmb: t.matmul(pcw[1][:, :], l, r, start=True, stop=True))
                    t_cwmm_prev = t_cwmm
                    S_.wait("scalar", "pe", t_cwmm)
                    for th in range(2):
                        t_cwev[ee] = S_.compute("scalar", "ac",
                                                lambda sc, o=cwb[:, ei * THALF + th * 512: ei * THALF + th * 512 + 512],
                                                i=pcw[th][:, :]: sc.copy(o, i))

                for ei, ee in enumerate(pair):
                    hb = A_h[:, ei * 6 * THALF:(ei + 1) * 6 * THALF]
                    for j in range(6):
                        if p == 0 and ei == 0 and j == 0:
                            for _ in range(4):
                                emit_wd_dma(wd_pf); wd_pf += 1
                        gbuf, ubuf = gu_buf(j, 0), gu_buf(j, 1)
                        S_.wait("tensor", "gp", gu_done[(ee, j)])
                        for gi, wb_, pgu, tdict in ((0, gbuf, pg, t_gmm), (1, ubuf, pu, t_umm)):
                            for th in range(2):
                                prev = (ee, j - 1, th) if j >= 1 else ((ee - 1, 5, th) if ee > 0 else None)
                                if prev is not None and prev in t_silu:
                                    if gi == 0:
                                        S_.wait("tensor", "ac", t_silu[prev])
                                    else:
                                        S_.wait("tensor", "ve", t_us[prev])
                                rhs0 = kt_rhs = None
                                for kt in range(HKT):
                                    rhs = A_ln[:, kt * TPC + T * THALF + th * 512: kt * TPC + T * THALF + th * 512 + 512]
                                    if kt < HKT - 1:
                                        S_.run("tensor", lambda t, o=pgu[th][:, :], l=wb_[:, kt * 128:(kt + 1) * 128], r=rhs,
                                               st=(kt == 0): t.matmul(o, l, r, start=st, stop=False))
                                    else:
                                        tdict[(ee, j, th)] = S_.compute(
                                            "tensor", "pe",
                                            lambda t, o=pgu[th][:, :], l=wb_[:, kt * 128:(kt + 1) * 128], r=rhs:
                                            t.matmul(o, l, r, start=False, stop=True))
                        if j < 5:
                            emit_gu_dma(gu_pf); gu_pf += 1
                        for th in range(2):
                            S_.wait("scalar", "pe", t_gmm[(ee, j, th)])
                            hprev = (ee, j - 1, th) if j >= 1 else ((ee - 1, 5, th) if ee > 0 else None)
                            if hprev is not None and hprev in t_h:
                                S_.wait("scalar", "ve", t_h[hprev])
                            t_silu[(ee, j, th)] = S_.compute("scalar", "ac",
                                                             lambda sc, o=sg_sb[:, th * 512:(th + 1) * 512], i=pg[th][:, :]:
                                                             sc.activation(o, i, AF.Silu))
                            S_.wait("vector", "pe", t_umm[(ee, j, th)])
                            S_.wait("vector", "ac", t_cwev[ee])
                            t_us[(ee, j, th)] = S_.compute("vector", "ve",
                                                           lambda v, o=us_sb[:, th * 512:(th + 1) * 512], i=pu[th][:, :],
                                                           c=cwb[:, ei * THALF + th * 512: ei * THALF + th * 512 + 512]:
                                                           v.scalar_tensor_tensor(o, i, 1.0, c, ALU.mult, ALU.mult))
                            S_.wait("vector", "ac", t_silu[(ee, j, th)])
                            t_h[(ee, j, th)] = S_.compute("vector", "ve",
                                                          lambda v, o=hb[:, j * THALF + th * 512: j * THALF + th * 512 + 512],
                                                          a=us_sb[:, th * 512:(th + 1) * 512],
                                                          b=sg_sb[:, th * 512:(th + 1) * 512]:
                                                          v.tensor_mul(o, a, b))
                    emit_gu_dma(gu_pf); gu_pf += 1
                S_.wait("tensor", "ve", t_h[(pair[1], 5, 1)])
                S_.wait("tensor", "ac", t_cwev[pair[1]])
                for o in range(16):
                    for ei, ee in enumerate(pair):
                        S_.wait("tensor", "gp", wd_done[(ee, o)])
                        if ei == 0:
                            if o >= 2:
                                S_.wait("tensor", "ve", t_xacc[(p, o - 2, 1)])
                            elif p > 0:
                                S_.wait("tensor", "ve", t_xacc[(p - 1, 14 + o, 1)])
                        wbuf = wd_buf(ee, o)
                        hb = A_h[:, ei * 6 * THALF:(ei + 1) * 6 * THALF]
                        for kt in range(6):
                            for th in range(2):
                                lm = (ei == 1 and kt == 5 and th == 1)
                                if not lm:
                                    S_.run("tensor", lambda t, o_=pdmap[(o % 2, th)][:, :],
                                           l=wbuf[:, kt * 128:(kt + 1) * 128],
                                           r=hb[:, kt * THALF + th * 512: kt * THALF + th * 512 + 512],
                                           st=(ei == 0 and kt == 0): t.matmul(o_, l, r, start=st, stop=False))
                                else:
                                    t_dmm[(p, o)] = S_.compute("tensor", "pe",
                                                               lambda t, o_=pdmap[(o % 2, th)][:, :],
                                                               l=wbuf[:, kt * 128:(kt + 1) * 128],
                                                               r=hb[:, kt * THALF + th * 512: kt * THALF + th * 512 + 512]:
                                                               t.matmul(o_, l, r, start=False, stop=True))
                    S_.wait("vector", "pe", t_dmm[(p, o)])
                    for th in range(2):
                        t_xacc[(p, o, th)] = S_.compute(
                            "vector", "ve",
                            lambda v, xo=x_half[:, o * THALF + th * 512: o * THALF + th * 512 + 512],
                            ps=pdmap[(o % 2, th)][:, :]: v.tensor_add(xo, xo, ps))
                    emit_wd_dma(wd_pf); wd_pf += 1
                    emit_wd_dma(wd_pf); wd_pf += 1
            S_.wait("sync", "ve", t_xacc[(NE // 2 - 1, 15, 1)])
            t_xs = 0
            for g4 in range(4):
                t_xs = S_.dma("sync", "sy",
                              dst.ap().rearrange("(kt p) t -> p kt t", p=128)[:, g4 * 4:(g4 + 1) * 4,
                                                                             T * THALF:(T + 1) * THALF],
                              x_half[:, g4 * 4 * THALF:(g4 + 1) * 4 * THALF].rearrange("p (k t) -> p k t", k=4))
            S_.wait("sync", "sy", t_xs)
            return

        def emit_qkv():
            pq = [pb[0], pb[1]]
            ptr_b = pb[2].ap().bitcast(BF16)
            t_wdma = {}
            t_mm = {}
            t_ev = {}
            t_vtr = {}
            t_vev = {}
            t_st = {}

            def wbuf(o):
                return A_wg[:, (o % 4) * 2048:(o % 4) * 2048 + 2048]

            def emit_wdma(o):
                if o >= 48:
                    return
                if o >= 4:
                    S_.wait("gpsimd", "pe", t_mm[(o - 4, NTT - 1)])
                t_wdma[o] = S_.dma("gpsimd", "gp", wbuf(o).rearrange("p (k n) -> p k n", k=HKT),
                                   qkv_t.ap()[o])

            for oo in range(4):
                emit_wdma(oo)
            for o in range(48):
                typ, head = o // 16, o % 16
                S_.wait("tensor", "gp", t_wdma[o])
                for tt in range(NTT):
                    bb, tof = tt // 2, (tt % 2) * 512
                    if (o, tt - 2) in t_ev:
                        S_.wait("tensor", "ac", t_ev[(o, tt - 2)])
                    elif o > 0 and tt < 2:
                        S_.wait("tensor", "ac", t_ev[(o - 1, 2 + tt)])
                    for kt in range(HKT):
                        rhs = A_ln[:, kt * TPC + tt * 512: kt * TPC + tt * 512 + 512]
                        if kt < HKT - 1:
                            S_.run("tensor", lambda t, l=wbuf(o)[:, kt * 128:(kt + 1) * 128], r=rhs,
                                   o_=pq[tt % 2][:, :], st=(kt == 0): t.matmul(o_, l, r, start=st, stop=False))
                        else:
                            t_mm[(o, tt)] = S_.compute("tensor", "pe",
                                                       lambda t, l=wbuf(o)[:, kt * 128:(kt + 1) * 128], r=rhs,
                                                       o_=pq[tt % 2][:, :]: t.matmul(o_, l, r, start=False, stop=True))
                    S_.wait("scalar", "pe", t_mm[(o, tt)])
                    if (o, tt - 2) in t_st:
                        S_.wait("scalar", "sy", t_st[(o, tt - 2)])
                    elif o > 0 and tt < 2:
                        S_.wait("scalar", "sy", t_st[(o - 1, 2 + tt)])
                    t_ev[(o, tt)] = S_.compute("scalar", "ac",
                                               lambda sc, o_=qk_st[:, (tt % 2) * 512:(tt % 2) * 512 + 512],
                                               i=pq[tt % 2][:, :]: sc.copy(o_, i))
                    if typ < 2:
                        dst = cc_in_q if typ == 0 else cc_in_k
                        S_.wait("sync", "ac", t_ev[(o, tt)])
                        t_st[(o, tt)] = S_.dma("sync", "sy",
                                               dst.ap()[(head * 2 + bb) * 128:(head * 2 + bb) * 128 + 128,
                                                        tof:tof + 512],
                                               qk_st[:, (tt % 2) * 512:(tt % 2) * 512 + 512])
                    else:
                        S_.wait("tensor", "ac", t_ev[(o, tt)])
                        if (o, tt - 2) in t_vev:
                            S_.wait("tensor", "ac", t_vev[(o, tt - 2)])
                        elif o > 32 and tt < 2:
                            S_.wait("tensor", "ac", t_vev[(o - 1, 2 + tt)])
                        for tb in range(4):
                            t_vtr[(o, tt)] = S_.compute("tensor", "pe",
                                                        lambda t, i=qk_st[:, (tt % 2) * 512 + tb * 128:
                                                                          (tt % 2) * 512 + tb * 128 + 128],
                                                        o_=ptr_b[:, tb * 128:(tb + 1) * 128]:
                                                        t.transpose(o_, i, id_b[:, :]))
                        S_.wait("scalar", "pe", t_vtr[(o, tt)])
                        if (o, tt - 2) in t_st:
                            S_.wait("scalar", "sy", t_st[(o, tt - 2)])
                        elif o > 32 and tt < 2:
                            S_.wait("scalar", "sy", t_st[(o - 1, 2 + tt)])
                        t_vev[(o, tt)] = S_.compute("scalar", "ac",
                                                    lambda sc, o_=vt_st[:, (tt % 2) * 512:(tt % 2) * 512 + 512],
                                                    i=ptr_b[:, 0:512]: sc.copy(o_, i))
                        S_.wait("sync", "ac", t_vev[(o, tt)])
                        for tb in range(4):
                            ttg = (tt % 2) * 4 + tb
                            t_st[(o, tt)] = S_.dma("sync", "sy",
                                                   cc_in_v.ap()[(head * 2 + bb) * 128:
                                                                (head * 2 + bb) * 128 + 128,
                                                                ttg * 128:(ttg + 1) * 128],
                                                   vt_st[:, (tt % 2) * 512 + tb * 128:(tt % 2) * 512 + tb * 128 + 128])
                emit_wdma(o + 4)
            S_.wait_qstate("sync", S_.qstate("sy"))
            return

        def emit_ag(pairs):
            n = 0
            for src_t, dst_t in pairs:
                S_.compute("gpsimd", "cc",
                           lambda g, s=src_t, d=dst_t: g.collective_compute(
                               "AllGather", ALU.bypass, replica_groups=[list(range(NCORES))],
                               ins=[s.ap().opt()], outs=[d.ap().opt()]))
                n += 1
            S_.wait("gpsimd", "cc", S_.cnt["cc"])
            return

        def emit_attention():
            t_dm = S_.dma("sync", "sy", dmask_sb[:, :], dmask.ap())
            pS = [pb[0], pb[1], pb[2], pb[3]]
            pO = [pb[4], pb[5]]
            pDen = pb[6]
            pB = pb[7]
            t_ld = {}
            t_exp = {}
            t_den = {}
            t_pv = {}
            t_pvq = {}
            t_scl = {}
            t_ost = {}
            t_rec = {}
            t_dred = {}
            last_rec = [0]
            last_bc = [0]
            last_scl = [0]
            last_dred = {}

            def emit_job_loads(j):
                if j >= 4:
                    return
                jb = j % 2
                bb, hh = j % 2, j // 2
                if j >= 2:
                    S_.wait("sync", "pe", t_pvq[(j - 2, 15)])
                def _dyndma(sy, semh, off_fn, src_t, patt, dst_ap):
                    sy.dma_start(out=dst_ap, in_=bass.AP(src_t, off_fn(), patt)).then_inc(semh, 16)
                for tens, buf in ((q_all, qbuf[jb]), (k_all, kbuf[jb])):
                    dst = buf[:, :].rearrange("p (s t) -> p s t", s=NCORES)
                    semh_d, _tok = S_.next_dma_sem("sy")
                    S_.run("sync", lambda sy, sh=semh_d, jj=j, s=tens, d=dst:
                           _dyndma(sy, sh, lambda: (val["pid"] * 4 + jj) * (128 * THALF),
                                   s, [[THALF, 128], [4096 * THALF, NCORES], [1, THALF]], d))
                dstv = vbuf[jb][:, :].rearrange("p (s td) -> p s td", s=NCORES)
                semh_d, _tok = S_.next_dma_sem("sy")
                S_.run("sync", lambda sy, sh=semh_d, jj=j, d=dstv:
                       _dyndma(sy, sh, lambda: (val["pid"] * 4 + (jj // 2) * 2 + (jj % 2)) * (128 * 1024),
                               v_all, [[1024, 128], [4096 * 1024, NCORES], [1, 1024]], d))
                t_ld[j] = S_.qstate("sy")

            emit_job_loads(0)
            emit_job_loads(1)
            S_.wait("vector", "sy", t_dm)

            blocks = []
            for j in range(4):
                for qt in range(16):
                    nkb = 4 * qt + 4
                    for kb in range(nkb):
                        blocks.append((j, qt, kb, nkb))
            NBLK = len(blocks)
            LA = 3

            def emit_epilogue(j, qt, ilast):
                S_.wait("tensor", "ve", last_rec[0])
                S_.wait("tensor", "ve", t_den[ilast])
                t_dred[(j, qt)] = last_dred[qt % 2] = S_.compute(
                    "tensor", "pe",
                    lambda t, r=den_sb[:, (qt % 2) * 512:(qt % 2) * 512 + 512]:
                    t.matmul(pDen[0:1, :], cpool[:, 0:1], r, start=True, stop=True))
                S_.wait("vector", "pe", t_dred[(j, qt)])
                S_.wait("vector", "pe", last_bc[0])
                S_.run("vector", lambda v: v.reciprocal(denr_sb[0:1, :], pDen[0:1, :]))
                t_rec[(j, qt)] = last_rec[0] = S_.compute("vector", "ve", lambda v: v.drain())
                S_.wait("tensor", "ve", t_rec[(j, qt)])
                S_.wait("tensor", "ve", last_scl[0])
                t_bc = last_bc[0] = S_.compute("tensor", "pe",
                                  lambda t: t.matmul(pB[:, :], cprow[0:1, 128:256],
                                                     denr_sb[0:1, :], start=True, stop=True))
                S_.wait("scalar", "pe", t_pvq[(j, qt)])
                if (j, qt - 2) in t_ost:
                    S_.wait("scalar", "sy", t_ost[(j, qt - 2)])
                elif j > 0 and qt < 2:
                    S_.wait("scalar", "sy", t_ost[(j - 1, 14 + qt)])
                t_oev = S_.compute("scalar", "ac",
                                   lambda sc, i=pO[qt % 2][:, :],
                                   o_=o_sb[:, (qt % 2) * 512:(qt % 2) * 512 + 512]: sc.copy(o_, i))
                S_.wait("vector", "ac", t_oev)
                S_.wait("vector", "pe", t_bc)
                t_scl[(j, qt)] = last_scl[0] = S_.compute("vector", "ve",
                                            lambda v, a=o_sb[:, (qt % 2) * 512:(qt % 2) * 512 + 512],
                                            o_=at_st[:, (qt % 2) * 512:(qt % 2) * 512 + 512]:
                                            v.tensor_mul(o_, a, pB[:, :]))
                S_.wait("sync", "ve", t_scl[(j, qt)])
                t_ost[(j, qt)] = S_.dma("sync", "sy",
                                        cc_in_a.ap()[j * 128:(j + 1) * 128, qt * 512:(qt + 1) * 512],
                                        at_st[:, (qt % 2) * 512:(qt % 2) * 512 + 512])
                if qt == 15:
                    emit_job_loads(j + 2)

            def emit_qk(i):
                j, qt, kb, nkb = blocks[i]
                jb = j % 2
                par = i % 4
                if kb == 0 and qt == 0:
                    S_.wait_qstate("tensor", t_ld[j])
                if i >= 4:
                    S_.wait("tensor", "ac", t_exp[i - 4])
                qsl = qbuf[jb][:, qt * 512:(qt + 1) * 512]
                ksl = kbuf[jb][:, kb * 128:(kb + 1) * 128]
                t_qk = S_.compute("tensor", "pe",
                                  lambda t, l=ksl, r=qsl, o_=pS[par][:, :]:
                                  t.matmul(o_, l, r, start=True, stop=True))
                r = kb - 4 * qt
                if r >= 0:
                    S_.wait("vector", "pe", t_qk)
                    t_m = S_.compute("vector", "ve",
                                     lambda v, o_=pS[par][:, :],
                                     m=dmask_sb[:, r * 512:(r + 1) * 512]:
                                     v.tensor_add(o_, o_, m))
                    S_.wait("scalar", "ve", t_m)
                else:
                    S_.wait("scalar", "pe", t_qk)
                if i >= 4:
                    S_.wait("scalar", "pe", t_pv[i - 4])
                    S_.wait("scalar", "ve", t_den[i - 4])
                t_exp[i] = S_.compute("scalar", "ac",
                                      lambda sc, i_=pS[par][:, :],
                                      o_=e_sb[:, par * 512:(par + 1) * 512]:
                                      sc.activation(o_, i_, AF.Exp, bias=cpool[:, 6:7], scale=SCALE))
                S_.wait("vector", "ac", t_exp[i])
                if kb == 0:
                    S_.wait("vector", "pe", last_dred.get(qt % 2, 0))
                    S_.run("vector", lambda v, d=den_sb[:, (qt % 2) * 512:(qt % 2) * 512 + 512]:
                           v.memset(d, 0.0))
                t_den[i] = S_.compute("vector", "ve",
                                      lambda v, d=den_sb[:, (qt % 2) * 512:(qt % 2) * 512 + 512],
                                      s=e_sb[:, par * 512:(par + 1) * 512]:
                                      v.tensor_add(d, d, s))

            def emit_pv(i):
                j, qt, kb, nkb = blocks[i]
                jb = j % 2
                par = i % 4
                if kb == 0:
                    if (j, qt - 2) in t_scl:
                        S_.wait("tensor", "ve", t_scl[(j, qt - 2)])
                    elif j > 0 and qt < 2:
                        S_.wait("tensor", "ve", t_scl[(j - 1, 14 + qt)])
                S_.wait("tensor", "ac", t_exp[i])
                vsl = vbuf[jb][:, kb * 128:(kb + 1) * 128]
                t_pv[i] = S_.compute("tensor", "pe",
                                     lambda t, l=vsl,
                                     r=e_sb[:, par * 512:(par + 1) * 512],
                                     o_=pO[qt % 2][:, :], st=(kb == 0):
                                     t.matmul(o_, l, r, start=st, stop=(kb == nkb - 1)))
                if kb == nkb - 1:
                    t_pvq[(j, qt)] = t_pv[i]
                    emit_epilogue(j, qt, i)

            for i in range(NBLK + LA):
                if i < NBLK:
                    emit_qk(i)
                if i >= LA:
                    emit_pv(i - LA)
            S_.wait_qstate("sync", S_.qstate("sy"))
            return

        def emit_wo():
            def _xdma(sy, semh):
                dst = A_ln[:, :].rearrange("p (g t) -> p g t", g=NH * B)
                sy.dma_start(out=dst, in_=bass.AP(a_all, val["pid"] * THALF,
                                                  [[S, 128], [128 * S, NH * B], [1, THALF]])).then_inc(semh, 16)
            semh_x, _tok = S_.next_dma_sem("sy")
            S_.run("sync", lambda sy: _xdma(sy, semh_x))
            t_ex = S_.qstate("sy")
            S_.wait_qstate("tensor", t_ex)

            pw = [pb[0], pb[1]]
            t_wdma = {}
            t_mm = {}
            t_add = {}
            t_xl = {}
            t_xs = {}

            def wobuf(o):
                return A_wo[:, (o % 4) * 2048:(o % 4) * 2048 + 2048]

            def emit_wodma(o):
                if o >= 16:
                    return
                if o >= 4:
                    S_.wait("gpsimd", "pe", t_mm[(o - 4, NTT - 1)])
                t_wdma[o] = S_.dma("gpsimd", "gp", wobuf(o).rearrange("p (k n) -> p k n", k=HKT),
                                   wo_t.ap()[o])

            for oo in range(4):
                emit_wodma(oo)
            for o in range(16):
                S_.wait("tensor", "gp", t_wdma[o])
                for tt in range(NTT):
                    if (o, tt) not in t_xl:
                        pass
                    if (o, tt - 2) in t_xs:
                        S_.wait("sync", "sy", t_xs[(o, tt - 2)])
                    elif o > 0 and tt < 2:
                        S_.wait("sync", "sy", t_xs[(o - 1, 2 + tt)])
                    t_xl[(o, tt)] = S_.dma("sync", "sy", xt_sb[:, (tt % 2) * 512:(tt % 2) * 512 + 512],
                                           xc.ap()[o * 128:(o + 1) * 128, tt * 512:(tt + 1) * 512])
                    if (o, tt - 2) in t_add:
                        S_.wait("tensor", "ve", t_add[(o, tt - 2)])
                    elif o > 0 and tt < 2:
                        S_.wait("tensor", "ve", t_add[(o - 1, 2 + tt)])
                    for kt in range(HKT):
                        rhs = A_ln[:, kt * TPC + tt * 512: kt * TPC + tt * 512 + 512]
                        if kt < HKT - 1:
                            S_.run("tensor", lambda t, l=wobuf(o)[:, kt * 128:(kt + 1) * 128], r=rhs,
                                   o_=pw[tt % 2][:, :], st=(kt == 0): t.matmul(o_, l, r, start=st, stop=False))
                        else:
                            t_mm[(o, tt)] = S_.compute("tensor", "pe",
                                                       lambda t, l=wobuf(o)[:, kt * 128:(kt + 1) * 128], r=rhs,
                                                       o_=pw[tt % 2][:, :]: t.matmul(o_, l, r, start=False, stop=True))
                    S_.wait("vector", "pe", t_mm[(o, tt)])
                    S_.wait("vector", "sy", t_xl[(o, tt)])
                    t_add[(o, tt)] = S_.compute("vector", "ve",
                                                lambda v, x_=xt_sb[:, (tt % 2) * 512:(tt % 2) * 512 + 512],
                                                p_=pw[tt % 2][:, :]: v.tensor_add(x_, x_, p_))
                    S_.wait("sync", "ve", t_add[(o, tt)])
                    t_xs[(o, tt)] = S_.dma("sync", "sy", xc.ap()[o * 128:(o + 1) * 128, tt * 512:(tt + 1) * 512],
                                           xt_sb[:, (tt % 2) * 512:(tt % 2) * 512 + 512])
                emit_wodma(o + 4)
            S_.wait_qstate("sync", S_.qstate("sy"))
            return

        if stop_after == "attn_only":
            emit_attention()
            stop_after = "done"
        if stop_after == "done":
            pass
        elif True:
            emit_ln(x0)
        S_.barrier()
        if stop_after == "done":
            pass
        elif "ln" in dbg:
            t_d = 0
            for kt in range(HKT):
                t_d = S_.dma("sync", "sy", dbg["ln"].ap()[kt * 128:(kt + 1) * 128, :], A_ln[:, kt * TPC:(kt + 1) * TPC])
            S_.wait("sync", "sy", t_d)
            S_.barrier()
        if stop_after == "done":
            pass
        elif stop_after != "ln1":
            emit_routing()
            S_.barrier()
            if "combine" in dbg:
                t_d = S_.dma("sync", "sy", dbg["combine"].ap()[:, :], combineT[:, :])
                S_.wait("sync", "sy", t_d)
                S_.barrier()
            if stop_after != "routing1":
                emit_ffn_half(0, x0, xc)
                S_.barrier()
                emit_ffn_half(1, x0, xc)
                S_.barrier()

                def _copy_xc_out():
                    t_o = 0
                    for g4 in range(4):
                        t_o = S_.dma("sync", "sy",
                                     out.ap().rearrange("(kt p) t -> p kt t", p=128)[:, g4 * 4:(g4 + 1) * 4, :],
                                     xc.ap().rearrange("(kt p) t -> p kt t", p=128)[:, g4 * 4:(g4 + 1) * 4, :])
                    S_.wait("sync", "sy", t_o)

                if stop_after == "ffn1":
                    _copy_xc_out()
                else:
                    emit_ln(xc)
                    S_.barrier()
                    emit_qkv()
                    S_.barrier()
                    if "qkv" in dbg:
                        S_.dma("sync", "sy", dbg["qkv"].ap()[:, :], cc_in_q.ap()[:, :])
                        S_.dma("sync", "sy", dbg["qkv_k"].ap()[:, :], cc_in_k.ap()[:, :])
                        t_d = S_.dma("sync", "sy", dbg["qkv_v"].ap()[:, :], cc_in_v.ap()[:, :])
                        S_.wait("sync", "sy", t_d)
                        S_.barrier()
                    if stop_after != "qkv":
                        emit_ag([(cc_in_q, q_all), (cc_in_k, k_all), (cc_in_v, v_all)])
                        S_.barrier()
                        emit_attention()
                        S_.barrier()
                        if "attn" in dbg:
                            t_d = S_.dma("sync", "sy", dbg["attn"].ap()[:, :], cc_in_a.ap()[:, :])
                            S_.wait("sync", "sy", t_d)
                            S_.barrier()
                        if stop_after != "attn":
                            emit_ag([(cc_in_a, a_all)])
                            S_.barrier()
                            emit_wo()
                            S_.barrier()
                            if stop_after == "wo":
                                _copy_xc_out()
                            else:
                                emit_ln(xc)
                                S_.barrier()
                                emit_routing()
                                S_.barrier()
                                emit_ffn_half(0, xc, out)
                                S_.barrier()
                                emit_ffn_half(1, xc, out)
        S_.barrier()

        block = e(nc.Block())

        @block.sync
        def _(sy):
            for fn in S_.ops["sync"]:
                fn(sy)

        @block.gpsimd
        def _(g):
            for fn in S_.ops["gpsimd"]:
                fn(g)

        @block.tensor
        def _(t):
            for fn in S_.ops["tensor"]:
                fn(t)

        @block.scalar
        def _(sc):
            for fn in S_.ops["scalar"]:
                fn(sc)

        @block.vector
        def _(v):
            for fn in S_.ops["vector"]:
                fn(v)

    return nc


def _prep_shared(w_qkv, w_o, gate_w, gup, dnp, sim):
    gu = np.ascontiguousarray(
        gup.transpose(0, 2, 1).reshape(NE, HKT, 128, 12, 128).transpose(0, 3, 2, 1, 4))
    dn = np.ascontiguousarray(
        dnp.transpose(0, 2, 1).reshape(NE, 6, 128, 16, 128).transpose(0, 3, 2, 1, 4))
    qkv = np.ascontiguousarray(w_qkv.T.reshape(HKT, 128, 48, 128).transpose(2, 1, 0, 3))
    wo = np.ascontiguousarray(w_o.T.reshape(HKT, 128, 16, 128).transpose(2, 1, 0, 3))
    gate = np.ascontiguousarray(gate_w.T.reshape(HKT, 128, NE).transpose(1, 0, 2))
    simc = np.ascontiguousarray(sim)

    cvals = np.zeros((128, 16), np.float32)
    CONSTS = [1.0, 0.5, 0.0, -1.0, 1.0 / H, EPS, -CEXP, SCALE, 2.0, -0.5, 0.25, 0.125, 3.0, 4.0, 5.0, 6.0]
    for i, cv in enumerate(CONSTS):
        cvals[:, i] = cv
    crow = np.zeros((1, 256), np.float32)
    crow[0, 0:128] = 0.5
    crow[0, 128:256] = 1.0
    iota = np.broadcast_to(np.arange(NE, dtype=np.float32), (128, NE)).copy()
    identm = np.eye(128, dtype=np.float32)
    dmaskv = np.zeros((128, 4 * 512), np.float32)
    ki = np.arange(128)[:, None]
    qi = np.arange(512)[None, :]
    for r in range(4):
        dmaskv[:, r * 512:(r + 1) * 512] = np.where(qi >= 128 * r + ki, 0.0, -1e30)
    dmaskv = dmaskv.astype(ml_dtypes.bfloat16)
    pcolv = np.arange(NE, dtype=np.float32).reshape(NE, 1)
    return dict(gu_t=gu, dn_t=dn, qkv_t=qkv, wo_t=wo, gate_t=gate, simm=simc,
                cvals=cvals, crow=crow, iota32=iota, ident=identm, dmask=dmaskv,
                pcol=pcolv)


def _shard_x(x):
    shards = []
    for c in range(NCORES):
        tok = np.concatenate([x[0, c * THALF:(c + 1) * THALF], x[1, c * THALF:(c + 1) * THALF]], axis=0)
        shards.append(np.ascontiguousarray(tok.T))
    return shards


def _unshard_x(outs):
    res = np.empty((B, S, H), np.float32)
    for c in range(NCORES):
        tok = outs[c].T
        res[0, c * THALF:(c + 1) * THALF] = tok[:THALF]
        res[1, c * THALF:(c + 1) * THALF] = tok[THALF:]
    return res


def kernel(L0_attn_hidden_states, L0_attn_w_qkv, L0_attn_w_o, L0_ffn_gate_weight,
           L0_ffn_gate_up_proj, L0_ffn_down_proj, L0_ffn_similarity,
           _stop_after=None, _dev_dump=()):
    shared = _prep_shared(np.asarray(L0_attn_w_qkv, np.float32), np.asarray(L0_attn_w_o, np.float32),
                          np.asarray(L0_ffn_gate_weight, np.float32),
                          np.asarray(L0_ffn_gate_up_proj, np.float32),
                          np.asarray(L0_ffn_down_proj, np.float32),
                          np.asarray(L0_ffn_similarity, np.float32))
    xsh = _shard_x(np.asarray(L0_attn_hidden_states, np.float32))
    nc = build_program(stop_after=_stop_after, dev_dump=_dev_dump)
    in_maps = [dict(shared, x0=xsh[c]) for c in range(NCORES)]
    res = run_bass_kernel_spmd(nc, in_maps, core_ids=list(range(NCORES)))
    if _dev_dump:
        return res
    return _unshard_x([res.results[c]["xout"] for c in range(NCORES)])
